# version 28
# baseline (speedup 1.0000x reference)
import numpy as np
import ml_dtypes
import concourse.bass as bass
import concourse.mybir as mybir
from concourse.bass_utils import run_bass_kernel_spmd

F32 = mybir.dt.float32
BF16 = mybir.dt.bfloat16
U8 = mybir.dt.uint8
AF = mybir.ActivationFunctionType
OP = mybir.AluOpType
AX = mybir.AxisListType

D = 2048
DC = D // 128
TOK = 2048
TS = 512
NTS = TOK // TS
HFF = 8192
GF = HFF // 128
HKV = 8
EPS_LN = 1e-5
EPS_RMS = float(np.finfo(np.float32).eps)


class Prog:

    def __init__(self, nc):
        self.nc = nc
        self.sem_h = {}
        self.cnt = {}
        self.last_wait = {}

    def make_sems(self, stack, names):
        for n in names:
            self.sem_h[n] = stack.enter_context(self.nc.semaphore(n))
            self.cnt[n] = 0

    def inc(self, name, instr, n=1):
        self.cnt[name] += n
        instr.then_inc(self.sem_h[name], n)
        return self.cnt[name]

    def wait(self, eng, name, val=None):
        v = self.cnt[name] if val is None else val
        if v <= 0:
            return
        key = (id(eng), name)
        if self.last_wait.get(key, -1) >= v:
            return
        eng.wait_ge(self.sem_h[name], v)
        self.last_wait[key] = v


class Arena:
    def __init__(self, tensor, nbytes):
        self.t = tensor
        self.nbytes = nbytes
        self.off = 0

    def reset(self):
        self.off = 0

    def carve(self, shape, dtype):
        isz = 2 if dtype == BF16 else 4
        n = shape[1] * isz
        off = (self.off + 63) & ~63
        assert off + n <= self.nbytes, f"arena overflow {off}+{n}>{self.nbytes}"
        v = self.t[:, off:off + n].bitcast(dtype)
        if shape[0] != 128:
            v = v[0:shape[0], :]
        self.off = off + n
        return v


class Ring:

    def __init__(self, p, stack, name, n):
        self.p = p
        self.names = [f"{name}{i}" for i in range(n)]
        p.make_sems(stack, self.names)
        self.i = 0
        self.n = n

    def issue(self, eng, out, in_):
        s = self.names[self.i % self.n]
        self.i += 1
        self.p.wait(eng, s)
        v = self.p.inc(s, eng.dma_start(out=out, in_=in_), 16)
        return (s, v)

    def wait(self, eng, tok):
        if tok is not None:
            self.p.wait(eng, tok[0], tok[1])

    def wait_all(self, eng):
        for s in self.names:
            self.p.wait(eng, s)


def build_program(dump=False):
    import contextlib
    nc = bass.Bass("TRN2", target_bir_lowering=False, debug=False)
    p = Prog(nc)

    x_in = nc.dram_tensor("x", [D, TOK], F32, kind="ExternalInput").ap()
    wq_t = nc.dram_tensor("wq_t", [D, D], F32, kind="ExternalInput").ap()
    wk_t = nc.dram_tensor("wk_t", [D, 512], F32, kind="ExternalInput").ap()
    wv_t = nc.dram_tensor("wv_t", [D, 512], F32, kind="ExternalInput").ap()
    wo_t = nc.dram_tensor("wo_t", [D, D], F32, kind="ExternalInput").ap()
    wqr_t = nc.dram_tensor("wqr_t", [D, D], F32, kind="ExternalInput").ap()
    wkr_t = nc.dram_tensor("wkr_t", [D, 512], F32, kind="ExternalInput").ap()
    bqr_r = nc.dram_tensor("bqr_r", [128, 16], F32, kind="ExternalInput").ap()
    bkr_r = nc.dram_tensor("bkr_r", [128, 4], F32, kind="ExternalInput").ap()
    bq_r = nc.dram_tensor("bq_r", [128, 16], F32, kind="ExternalInput").ap()
    bk_r = nc.dram_tensor("bk_r", [128, 4], F32, kind="ExternalInput").ap()
    bv_r = nc.dram_tensor("bv_r", [128, 4], F32, kind="ExternalInput").ap()
    bo_r = nc.dram_tensor("bo_r", [128, 16], F32, kind="ExternalInput").ap()
    wgu_in = nc.dram_tensor("wgu", [2 * HFF, D], F32, kind="ExternalInput").ap()
    wpr_in = nc.dram_tensor("wpr", [D, HFF], F32, kind="ExternalInput").ap()
    cosr = nc.dram_tensor("cosr", [128, TOK], F32, kind="ExternalInput").ap()
    sinr = nc.dram_tensor("sinr", [128, TOK], F32, kind="ExternalInput").ap()
    ident_in = nc.dram_tensor("ident", [128, 128], BF16, kind="ExternalInput").ap()
    ident64_in = nc.dram_tensor("ident64", [128, 64], BF16, kind="ExternalInput").ap()
    y_out = nc.dram_tensor("y", [D, TOK], F32, kind="ExternalOutput").ap()

    dump = dump or set()
    if dump is True:
        dump = {"s_x1", "s_x2"}
    def dk(name):
        return "ExternalOutput" if name in dump else "Internal"
    s_x1 = nc.dram_tensor("s_x1", [D, TOK], F32, kind=dk("s_x1")).ap()
    s_x2 = nc.dram_tensor("s_x2", [D, TOK], F32, kind=dk("s_x2")).ap()
    s_h = nc.dram_tensor("s_h", [HFF, TOK], BF16, kind=dk("s_h")).ap()
    s_wgu = nc.dram_tensor("s_wgu", [2 * HFF, D], BF16, kind=dk("s_wgu")).ap()
    s_wpr = nc.dram_tensor("s_wpr", [HFF, D], BF16, kind=dk("s_wpr")).ap()
    s_awt = nc.dram_tensor("s_awt", [60 * D, 128], BF16).ap()
    s_q = nc.dram_tensor("s_q", [D, TOK], BF16, kind=dk("s_q")).ap()
    s_kl = nc.dram_tensor("s_kl", [512, TOK], BF16, kind=dk("s_kl")).ap()
    s_vl = nc.dram_tensor("s_vl", [TOK, HKV * 65], BF16, kind=dk("s_vl")).ap()
    s_ka = nc.dram_tensor("s_ka", [1024, TOK], BF16).ap()
    s_va = nc.dram_tensor("s_va", [2 * TOK, HKV * 65], BF16).ap()
    s_bc = nc.dram_tensor("s_bc", [8, 4096], F32, kind=dk("s_bc")).ap()

    stack = contextlib.ExitStack()
    with stack:
        bcA = stack.enter_context(nc.sbuf_tensor("bcA", [128, TOK], F32))
        bcB = stack.enter_context(nc.sbuf_tensor("bcB", [128, TOK], F32))
        consts = stack.enter_context(nc.sbuf_tensor("consts", [128, 1024], U8))
        ident = stack.enter_context(nc.sbuf_tensor("identt", [128, 128], BF16))
        ident64 = stack.enter_context(nc.sbuf_tensor("identt64", [128, 64], BF16))
        ws = stack.enter_context(nc.sbuf_tensor("ws", [128, 8192], U8))
        w0 = ws.ap().bitcast(F32)
        bq4 = consts[:, 0:64].bitcast(F32)
        bqr4 = consts[:, 512:576].bitcast(F32)
        bkr4 = consts[:, 576:592].bitcast(F32)
        bqr8 = consts[:, 592:656].bitcast(F32)
        bq8 = consts[:, 64:128].bitcast(F32)
        bk4 = consts[:, 128:144].bitcast(F32)
        bv4 = consts[:, 144:160].bitcast(F32)
        bo4 = consts[:, 160:224].bitcast(F32)
        ones16 = consts[:, 224:226].bitcast(BF16)
        ones64f = consts[:, 768:1024].bitcast(F32)
        onesf = consts[:, 228:232].bitcast(F32)
        negcb = consts[:, 232:236].bitcast(F32)
        nrow = consts[:, 256:512].bitcast(F32)
        mq2 = nrow[0:1, 0:32]
        mk2 = nrow[0:1, 32:40]

        arena_t = stack.enter_context(nc.sbuf_tensor("arena", [128, 182 * 1024], U8))
        AR = Arena(arena_t, 182 * 1024)

        psA_t = stack.enter_context(nc.psum_tensor("psA", [128, 8192], U8))
        psB_t = stack.enter_context(nc.psum_tensor("psB", [128, 8192], U8))
        psA = psA_t.ap().bitcast(F32)
        psB = psB_t.ap().bitcast(F32)
        psAh = psA_t.ap().bitcast(BF16)
        psBh = psB_t.ap().bitcast(BF16)

        def fA(i):
            return psA[:, 512 * i:512 * (i + 1)]

        def fB(i):
            return psB[:, 512 * i:512 * (i + 1)]

        p.make_sems(stack, ["tpe", "tdve", "tact", "tpool", "cc"])
        w32war = {0: 0, 1: 0, 2: 0}
        sync, ten, vec, act, gp = nc.sync, nc.tensor, nc.vector, nc.scalar, nc.gpsimd
        rS = Ring(p, stack, "rS", 12)
        rG = Ring(p, stack, "rG", 4)

        def dma(queue, sem, out, in_):
            return (rG if queue == "B" else rS).issue(gp if queue == "B" else sync,
                                                      out, in_)

        def dwait_all(eng):
            rS.wait_all(eng)
            rG.wait_all(eng)

        def phase_barrier():
            for eng in (sync, ten, vec, act, gp):
                for s in ("tpe", "tdve", "tact", "tpool", "cc"):
                    p.wait(eng, s)
                dwait_all(eng)

        dma("A", "dA", bq4, bq_r[:, :])
        dma("A", "dA", bk4, bk_r[:, :])
        dma("A", "dA", bv4, bv_r[:, :])
        dma("A", "dA", bo4, bo_r[:, :])
        dma("A", "dA", bqr4, bqr_r[:, :])
        dma("A", "dA", bkr4, bkr_r[:, :])
        dma("A", "dA", ident[:, :], ident_in[:, :])
        dma("A", "dA", ident64[:, :], ident64_in[:, :])
        dwait_all(vec)
        p.inc("tdve", vec.memset(ones16, 1.0))
        p.inc("tdve", vec.memset(onesf, 1.0))
        p.inc("tdve", vec.memset(ones64f[0:1, :], 1.0))
        p.inc("tdve", vec.tensor_scalar_mul(bq8, bq4, 0.125))
        p.inc("tdve", vec.tensor_scalar_mul(bqr8, bqr4, 0.125))

        AR.reset()
        cw32 = [AR.carve([128, 2048], F32) for _ in range(2)]
        cw16 = [AR.carve([128, 2048], BF16) for _ in range(2)]
        par = 0
        awt3 = s_awt.rearrange("(s r) d -> s r d", r=D)
        for (w_ap, ncols, st_base) in ((wq_t, D, 0), (wk_t, 512, 16),
                                       (wv_t, 512, 20), (wo_t, D, 24),
                                       (wqr_t, D, 40), (wkr_t, 512, 56)):
            for c in range(DC):
                w32v, w16v = cw32[par], cw16[par]
                dwait_all(sync)
                p.wait(sync, "tact")
                dma("A", "dA", w32v[:, 0:ncols], w_ap[128 * c:128 * (c + 1), :])
                dwait_all(act)
                p.inc("tact", act.copy(out=w16v[:, 0:ncols], in_=w32v[:, 0:ncols]))
                p.wait(sync, "tact")
                dma("A", "dA",
                    awt3[st_base:st_base + ncols // 128,
                         128 * c:128 * (c + 1), :].transpose([1, 0, 2]),
                    w16v[:, 0:ncols].rearrange("p (s d) -> p s d", d=128))
                par ^= 1

        def ln_xln(xt, ts, sqbufs, xln_slice, dsem, rb=0):
            mm_after_sq = [0, 0]
            for c in range(DC):
                sq = sqbufs[c % 2]
                if c >= 2:
                    p.wait(vec, "tpe", mm_after_sq[c % 2])
                p.inc("tdve", vec.tensor_mul(sq, xt[:, 512 * c:512 * (c + 1)],
                                             xt[:, 512 * c:512 * (c + 1)]))
                p.wait(ten, "tdve")
                p.inc("tpe", ten.matmul(fA(0)[0:1, :], onesf,
                                        xt[:, 512 * c:512 * (c + 1)],
                                        start=(c == 0), stop=(c == DC - 1)))
                mm_after_sq[c % 2] = p.inc(
                    "tpe", ten.matmul(fA(1)[0:1, :], onesf, sq,
                                      start=(c == 0), stop=(c == DC - 1)))
            p.wait(vec, "tpe")
            dwait_all(vec)
            p.inc("tdve", vec.tensor_scalar_mul(w0[0:1, 0:512], fA(0)[0:1, :], 1.0 / D))
            p.inc("tdve", vec.tensor_scalar_mul(w0[0:1, 512:1024], fA(1)[0:1, :], 1.0 / D))
            p.wait(vec, "tdve")
            p.inc("tdve", vec.tensor_mul(w0[0:1, 1024:1536], w0[0:1, 0:512], w0[0:1, 0:512]))
            p.wait(vec, "tdve")
            p.inc("tdve", vec.tensor_tensor(out=w0[0:1, 1536:2048], in0=w0[0:1, 512:1024],
                                            in1=w0[0:1, 1024:1536], op=OP.subtract))
            p.wait(vec, "tdve")
            p.inc("tdve", vec.tensor_scalar_add(w0[0:1, 1024:1536],
                                                w0[0:1, 1536:2048], EPS_LN))
            p.wait(act, "tdve")
            p.inc("tact", act.activation(out=w0[0:1, 1536:2048],
                                         in_=w0[0:1, 1024:1536], func=AF.Ln))
            p.wait(act, "tact")
            p.inc("tact", act.activation(out=w0[0:1, 1024:1536],
                                         in_=w0[0:1, 1536:2048], func=AF.Exp, scale=-0.5))
            p.wait(vec, "tact")
            p.inc("tdve", vec.tensor_mul(w0[0:1, 1536:2048],
                                         w0[0:1, 0:512], w0[0:1, 1024:1536]))
            p.wait(sync, "tdve")
            dma("A", dsem, s_bc[rb:rb + 1, 512 * ts:512 * (ts + 1)], w0[0:1, 1024:1536])
            dma("A", dsem, s_bc[rb + 1:rb + 2, 512 * ts:512 * (ts + 1)], w0[0:1, 1536:2048])
            dwait_all(sync)
            dma("A", dsem, bcA[:, 512 * ts:512 * (ts + 1)],
                s_bc[rb:rb + 1, 512 * ts:512 * (ts + 1)].broadcast_to([128, 512]))
            dma("A", dsem, bcB[:, 512 * ts:512 * (ts + 1)],
                s_bc[rb + 1:rb + 2, 512 * ts:512 * (ts + 1)].broadcast_to([128, 512]))
            dwait_all(vec)
            for c in range(DC):
                t0 = sqbufs[c % 2]
                p.wait(vec, "tdve")
                p.inc("tdve", vec.tensor_mul(t0, xt[:, 512 * c:512 * (c + 1)],
                                             bcA[:, 512 * ts:512 * (ts + 1)]))
                p.wait(vec, "tdve")
                p.inc("tdve", vec.tensor_tensor(out=xln_slice(c), in0=t0,
                                                in1=bcB[:, 512 * ts:512 * (ts + 1)],
                                                op=OP.subtract))

        G3 = lambda apx: apx.rearrange("p (g j) -> p g j", j=64)

        def q_load(qt, src_slice, dsem, war_tpe=None):
            p.wait(sync, "tact",
                   w32war.get(qt.get("w32_idx", 0), 0))
            if war_tpe is None:
                p.wait(sync, "tdve")
                p.wait(sync, "tpe")
            else:
                p.wait(sync, "tpe", war_tpe)
                p.wait(sync, "tdve", qt.get("rd_w16", 0))
            qt["_w32_tok"] = dma("A", dsem, qt["w32"][:, :], src_slice)
            qt["_war_tpe"] = war_tpe

        def q_cast(qt):
            rS.wait(act, qt["_w32_tok"])
            p.wait(act, "tdve", qt.get("rd_w16", 0))
            if qt["_war_tpe"] is not None:
                p.wait(act, "tpe", qt["_war_tpe"])
                p.wait(vec, "tpe", qt["_war_tpe"])
            p.wait(act, "tpool", qt.get("_gp_done", 0))
            w32war[qt.get("w32_idx", 0)] = p.inc(
                "tact", act.copy(out=qt["w16"][:, :], in_=qt["w32"][:, :]))
            qt["_cast"] = p.cnt["tact"]

        def q_front(qt):
            t16 = qt["aw"]
            p.wait(vec, "tact", qt["_cast"])
            p.inc("tdve", vec.reduce_sum(out=qt["ssum"], in_=G3(qt["w16"][:, :]),
                                         axis=AX.X, apply_absolute_value=True))
            p.wait(vec, "tdve")
            p.inc("tdve", vec.tensor_scalar(out=qt["s16"], in0=qt["ssum"],
                                            scalar1=1.0 / 64, scalar2=1e-8,
                                            op0=OP.mult, op1=OP.max))
            p.wait(vec, "tdve")
            rcp_done = p.inc("tdve", vec.reciprocal(out=qt["r32"], in_=qt["s16"]))
            p.wait(act, "tdve", rcp_done)
            p.wait(act, "tdve", qt.get("_s16f_war", 0))
            qt["_s16f"] = p.inc("tact", act.copy(
                out=G3(qt["s16f"][:, :]),
                in_=qt["s16"][:, :, None].broadcast_to([128, 32, 64])))
            p.wait(gp, "tdve", rcp_done)
            p.wait(gp, "tact", qt["_cast"])
            qt["_tq"] = p.inc("tpool", gp.tensor_tensor(
                out=G3(t16[:, :]), in0=G3(qt["w16"][:, :]),
                in1=qt["r32"][:, :, None].broadcast_to([128, 32, 64]), op=OP.mult))
            qt["_gp_done"] = qt["_tq"]

        def quant_back(qt):
            t16 = qt["aw"]
            p.wait(vec, "tpool", qt["_tq"])
            p.inc("tdve", vec.tensor_scalar(
                out=qt["qa"][:, :], in0=t16[:, :], scalar1=0.5, scalar2=None,
                op0=OP.is_gt))
            p.wait(vec, "tpe", qt.get("t_tpe", 0))
            p.inc("tdve", vec.tensor_scalar(
                out=qt["qb"][:, :], in0=t16[:, :], scalar1=-0.5, scalar2=None,
                op0=OP.is_lt))
            p.wait(vec, "tdve")
            p.inc("tdve", vec.tensor_sub(t16[:, :], qt["qa"][:, :],
                                         qt["qb"][:, :]))
            p.wait(vec, "tdve")
            p.wait(vec, "tact", qt["_s16f"])
            qt["_s16f_war"] = p.inc("tdve", vec.tensor_tensor(
                out=qt["qb"][:, :], in0=t16[:, :],
                in1=qt["s16f"][:, :], op=OP.mult))
            p.wait(vec, "tdve")
            p.inc("tdve", vec.tensor_sub(qt["d16"][:, :], qt["qb"][:, :],
                                         qt["w16"][:, :]))
            p.wait(vec, "tdve")
            p.inc("tdve", vec.tensor_add(qt["wq16"][:, :], qt["w16"][:, :], qt["d16"][:, :]))
            qt["rd_w16"] = p.cnt["tdve"]

        tr_state = {"evac": [0, 0]}

        def transpose_tile(src16, dst_buf, war_tpe=None, war_tok=None,
                           src_tdve=None):
            if src_tdve is None:
                src_tdve = p.cnt["tdve"]
            for half in range(2):
                bank = psBh[:, 2048 + 1024 * half: 2048 + 1024 * (half + 1)]
                p.wait(ten, "tact", tr_state["evac"][half])
                p.wait(ten, "tdve", src_tdve)
                for k in range(8):
                    blk = 8 * half + k
                    p.inc("tpe", ten.transpose(
                        bank[:, 128 * k:128 * (k + 1)],
                        src16[:, 128 * blk:128 * (blk + 1)], ident[:, :]))
                t_end = p.cnt["tpe"]
                p.wait(act, "tpe", t_end)
                if war_tpe is not None:
                    p.wait(act, "tpe", war_tpe)
                if war_tok is not None:
                    rS.wait(act, war_tok)
                tr_state["evac"][half] = p.inc("tact", act.copy(
                    out=dst_buf[:, 1024 * half:1024 * (half + 1)], in_=bank))

        def ffn(src, dst, is_ffn1, dsem, dst_sem):
            AR.reset()
            xln = AR.carve([128, DC * TOK], BF16)
            mark = AR.off
            xt = AR.carve([128, 16 * 512], F32)
            sqbufs = (AR.carve([128, 512], F32), AR.carve([128, 512], F32))
            AR.off = mark
            w32_bufs = [AR.carve([128, 2048], F32) for _ in range(3)]
            qsets = []
            for i in range(3):
                qt = {"w32": w32_bufs[i], "w32_idx": i}
                for nm in ("w16", "aw", "qa", "qb", "s16f"):
                    qt[nm] = AR.carve([128, 2048], BF16)
                qt["d16"] = qt["qa"]
                qt["wq16"] = qt["qb"]
                qt["ssum"] = AR.carve([128, 32], F32)
                qt["s16"] = AR.carve([128, 32], BF16)
                qt["r32"] = AR.carve([128, 32], F32)
                qsets.append(qt)
            lg = [AR.carve([128, 2048], BF16) for _ in range(4)]
            lw = [AR.carve([128, 2048], BF16) for _ in range(2)]
            sg32 = [AR.carve([128, 512], F32) for _ in range(2)]
            h16 = [AR.carve([128, 512], BF16) for _ in range(2)]
            sq16 = [AR.carve([128, 512], BF16) for _ in range(2)]

            src3 = src.rearrange("(c p) t -> p c t", p=128)
            phase_barrier()

            for ts in range(NTS):
                p.wait(sync, "tdve")
                p.wait(sync, "tpe")
                dma("A", dsem, xt.rearrange("p (c t) -> p c t", c=DC),
                    src3[:, :, TS * ts:TS * (ts + 1)])
                dwait_all(vec)
                dwait_all(ten)
                ln_xln(xt, ts, sqbufs,
                       lambda c, ts=ts: xln[:, TOK * c + TS * ts: TOK * c + TS * ts + TS],
                       dsem)

            lg_ready = [0, 0]
            mm_par = [0, 0]
            ssq_rd = [0, 0]
            ssq_pend = None
            sil_done = [0, 0]
            h_done = [0, 0]
            hstore_tok = [None, None]
            lw_tok = [None, None]
            qst = {"call": 0}

            def ssq_bank(tsx):
                pp = 32 * (tsx % 2)
                return psB[pp:pp + 1, 512 * (tsx // 2):512 * (tsx // 2) + 512]

            def q_srcs(gq):
                return (wgu_in[128 * gq:128 * (gq + 1), :],
                        wgu_in[128 * (gq + GF):128 * (gq + GF + 1), :],
                        wpr_in[128 * (gq // 4):128 * (gq // 4 + 1),
                               2048 * (gq % 4):2048 * (gq % 4 + 1)])

            def q_transpose(qt, dst, war_tpe, war_tok=None):
                transpose_tile(qt["wq16"], dst, war_tpe=war_tpe, war_tok=war_tok,
                               src_tdve=qt.get("rd_w16"))
                qt["t_tpe"] = p.cnt["tpe"]

            def emit_loads_g(gq):
                for si, sl in enumerate(q_srcs(gq)):
                    q_load(qsets[si], sl, dsem, war_tpe=qsets[si].get("t_tpe"))

            def emit_casts_g(gq):
                for si in range(3):
                    q_cast(qsets[si])

            def emit_quant_g(gq):
                qts = (qsets[0], qsets[1], qsets[2])
                for qt in qts:
                    q_front(qt)
                for qt in qts:
                    quant_back(qt)
                return qts

            def emit_transpose_g(gq, qg, qu, qw):
                gp1 = gq % 2
                q_transpose(qg, lg[2 * gp1], mm_par[gp1])
                p.wait(sync, "tact")
                dma("A", dsem, s_wgu[128 * gq:128 * (gq + 1), :], lg[2 * gp1][:, :])
                q_transpose(qu, lg[2 * gp1 + 1], mm_par[gp1])
                p.wait(sync, "tact")
                dma("A", dsem, s_wgu[128 * (gq + GF):128 * (gq + GF + 1), :],
                    lg[2 * gp1 + 1][:, :])
                q_transpose(qw, lw[gp1], None, war_tok=lw_tok[gp1])
                p.wait(sync, "tact")
                e, s4 = gq // 4, gq % 4
                lw_tok[gp1] = dma(
                    "A", dsem,
                    s_wpr.rearrange("(j p) d -> p j d", p=128)[
                        :, 16 * s4:16 * (s4 + 1), 128 * e:128 * (e + 1)],
                    lw[gp1][:, :].rearrange("p (j d) -> p j d", d=128))
                lg_ready[gp1] = p.cnt["tact"]

            if is_ffn1:
                emit_loads_g(0)
                emit_casts_g(0)
                emit_loads_g(1)
                qg, qu, qw = emit_quant_g(0)
                emit_casts_g(1)
                emit_transpose_g(0, qg, qu, qw)
            for g in range(GF):
                gpar = g % 2
                lga, lgb = lg[2 * gpar], lg[2 * gpar + 1]
                if is_ffn1:
                    if g + 2 < GF:
                        emit_loads_g(g + 2)
                    nxt = tuple(qsets) if g + 1 < GF else None
                    if nxt is not None:
                        q_front(qsets[0])
                        q_front(qsets[1])
                    ld0 = ld_done = None
                else:
                    if g <= 1:
                        p.wait(sync, "tpe")
                        p.wait(sync, "tdve")
                    else:
                        p.wait(sync, "tpe", mm_par[gpar])
                    ld0 = dma("A", dsem, lga[:, :], s_wgu[128 * g:128 * (g + 1), :])
                    ld_done = dma("A", dsem, lgb[:, :],
                                  s_wgu[128 * (g + GF):128 * (g + GF + 1), :])
                for tsx in range(4):
                    pb2 = 2 * (tsx % 2)
                    gbank, ubank = fA(pb2), fA(pb2 + 1)
                    pr = tsx % 2
                    if is_ffn1:
                        p.wait(ten, "tact", lg_ready[gpar])
                    else:
                        rS.wait(ten, ld0)
                        rS.wait(ten, ld_done)
                    p.wait(ten, "tact", sil_done[pr])
                    p.wait(ten, "tdve", h_done[pr])
                    for c in range(DC):
                        for half, lbuf in ((0, lga), (1, lgb)):
                            p.inc("tpe", ten.matmul(
                                fA(pb2 + half),
                                lbuf[:, 128 * c:128 * (c + 1)],
                                xln[:, TOK * c + TS * tsx: TOK * c + TS * (tsx + 1)],
                                start=(c == 0), stop=(c == DC - 1)))
                    mm_par[gpar] = p.cnt["tpe"]
                    mm_ts = p.cnt["tpe"]
                    p.wait(act, "tpe", mm_ts)
                    p.wait(act, "tdve", h_done[pr])
                    rS.wait(act, hstore_tok[pr])
                    sil_done[pr] = p.inc("tact", act.activation(
                        out=sg32[pr], in_=gbank, func=AF.Silu))
                    p.wait(vec, "tact", sil_done[pr])
                    h_done[pr] = p.inc("tdve", vec.scalar_tensor_tensor(
                        out=h16[pr], in0=ubank, scalar=1.0, in1=sg32[pr],
                        op0=OP.mult, op1=OP.mult))
                    p.wait(act, "tdve", h_done[pr])
                    p.wait(act, "tpe", ssq_rd[pr])
                    sq_done = p.inc("tact", act.activation(
                        out=sq16[pr], in_=h16[pr], func=AF.Square))
                    if ssq_pend is not None:
                        pg, ptsx, psq = ssq_pend
                        p.wait(ten, "tact", psq)
                        ssq_rd[ptsx % 2] = p.inc(
                            "tpe", ten.matmul(ssq_bank(ptsx), ones16,
                                              sq16[ptsx % 2],
                                              start=(pg == 0),
                                              stop=(pg == GF - 1),
                                              tile_position=(0, 32 * (ptsx % 2))))
                    ssq_pend = (g, tsx, sq_done)
                    p.wait(sync, "tdve", h_done[pr])
                    hstore_tok[pr] = dma(
                        "A", dsem,
                        s_h[128 * g:128 * (g + 1), TS * tsx:TS * (tsx + 1)],
                        h16[pr])
                    if is_ffn1 and nxt is not None:
                        if tsx == 0:
                            q_front(qsets[2])
                        elif tsx == 1:
                            quant_back(qsets[0])
                            quant_back(qsets[1])
                        elif tsx == 2:
                            quant_back(qsets[2])
                            if g + 2 < GF:
                                emit_casts_g(g + 2)
                if is_ffn1 and nxt is not None:
                    emit_transpose_g(g + 1, *nxt)

            if ssq_pend is not None:
                pg, ptsx, psq = ssq_pend
                p.wait(ten, "tact", psq)
                p.inc("tpe", ten.matmul(ssq_bank(ptsx), ones16, sq16[ptsx % 2],
                                        start=False, stop=True,
                                        tile_position=(0, 32 * (ptsx % 2))))
                ssq_pend = None

            p.wait(vec, "tpe")
            dwait_all(vec)
            for tsx in range(4):
                p.inc("tdve", vec.tensor_scalar(
                    out=w0[0:1, 512 * tsx:512 * (tsx + 1)], in0=ssq_bank(tsx),
                    scalar1=1.0 / HFF, scalar2=EPS_RMS, op0=OP.mult, op1=OP.add))
            p.wait(act, "tdve")
            p.inc("tact", act.activation(out=w0[0:1, 0:2048], in_=w0[0:1, 0:2048],
                                         func=AF.Ln))
            p.wait(act, "tact")
            p.inc("tact", act.activation(out=w0[0:1, 0:2048], in_=w0[0:1, 0:2048],
                                         func=AF.Exp, scale=-0.5))
            p.wait(vec, "tact")
            p.inc("tdve", vec.tensor_scalar_mul(w0[0:1, 0:2048], w0[0:1, 0:2048], 0.5))
            p.wait(sync, "tdve")
            dma("A", dsem, s_bc[2:3, 0:2048], w0[0:1, 0:2048])
            dwait_all(sync)
            dma("A", dsem, bcA[:, :], s_bc[2:3, 0:2048].broadcast_to([128, 2048]))

            phase_barrier()
            AR.off = 0
            wph = AR.carve([128, GF * 1024], BF16)
            NHB = 6
            hb = [AR.carve([128, 512], BF16) for _ in range(NHB)]
            xb = [AR.carve([128, 512], F32) for _ in range(2)]
            t32 = [AR.carve([128, 512], F32) for _ in range(2)]
            mm_hc = [0] * 6
            stc_tok = [None, None]
            ep_done = p.cnt["tdve"]
            p.wait(ten, "tact")
            for dtg in range(2):
                p.wait(sync, "tpe")
                wph_toks = []
                for hc in range(GF):
                    wph_toks.append(dma(
                        "A", dsem, wph[:, 1024 * hc:1024 * (hc + 1)],
                        s_wpr[128 * hc:128 * (hc + 1), 1024 * dtg:1024 * (dtg + 1)]))
                for ts in range(NTS):
                    for hc in range(GF):
                        pr = hc % NHB
                        p.wait(sync, "tpe", mm_hc[pr])
                        hld = dma("A", dsem, hb[pr],
                                  s_h[128 * hc:128 * (hc + 1), TS * ts:TS * (ts + 1)])
                        rS.wait(ten, hld)
                        if ts == 0:
                            rS.wait(ten, wph_toks[hc])
                        if hc == 0:
                            p.wait(ten, "tdve", ep_done)
                        for d8 in range(8):
                            p.inc("tpe", ten.matmul(
                                (fA(d8) if d8 < 4 else fB(d8 - 4)),
                                wph[:, 1024 * hc + 128 * d8:1024 * hc + 128 * (d8 + 1)],
                                hb[pr], start=(hc == 0), stop=(hc == GF - 1)))
                        mm_hc[pr] = p.cnt["tpe"]
                    for d8 in range(8):
                        dt = 8 * dtg + d8
                        pr = d8 % 2
                        p.wait(gp, "tdve")
                        xld = dma("B", dsem, xb[pr],
                                  src[128 * dt:128 * (dt + 1), TS * ts:TS * (ts + 1)])
                        p.wait(vec, "tpe")
                        rG.wait(vec, xld)
                        rG.wait(vec, stc_tok[pr])
                        p.inc("tdve", vec.tensor_mul(
                            t32[pr], (fA(d8) if d8 < 4 else fB(d8 - 4)),
                            bcA[:, 512 * ts:512 * (ts + 1)]))
                        p.wait(vec, "tdve")
                        p.inc("tdve", vec.tensor_add(t32[pr], t32[pr], xb[pr]))
                        p.wait(gp, "tdve")
                        stc_tok[pr] = dma("B", dst_sem,
                                          dst[128 * dt:128 * (dt + 1),
                                              TS * ts:TS * (ts + 1)], t32[pr])
                    ep_done = p.cnt["tdve"]

        ffn(x_in, s_x1, True, "dF", "dF")

        phase_barrier()
        AR.reset()
        xln2 = AR.carve([128, DC * TOK], BF16)
        mark2 = AR.off
        xt2 = AR.carve([128, 16 * 512], F32)
        sqb2 = (AR.carve([128, 512], F32), AR.carve([128, 512], F32))
        x13 = s_x1.rearrange("(c p) t -> p c t", p=128)
        dwait_all(sync)
        for ts in range(NTS):
            p.wait(sync, "tpe")
            p.wait(sync, "tdve")
            dma("A", "dA", xt2.rearrange("p (c t) -> p c t", c=DC),
                x13[:, :, TS * ts:TS * (ts + 1)])
            dwait_all(vec)
            dwait_all(ten)
            ln_xln(xt2, ts, sqb2,
                   lambda c, ts=ts: xln2[:, TOK * c + TS * ts: TOK * c + TS * ts + TS],
                   "dA", rb=5)
        xln_done = p.cnt["tdve"]

        phase_barrier()
        AR.off = mark2
        wsup = [AR.carve([128, 2048], BF16) for _ in range(2)]
        wsupr = [AR.carve([128, 2048], BF16) for _ in range(2)]
        cosb = AR.carve([128, TOK], F32)
        sinb = AR.carve([128, TOK], F32)
        cs_tok = [dma("A", "dA", cosb[:, :], cosr[:, :]),
                  dma("A", "dA", sinb[:, :], sinr[:, :])]
        qs32 = [AR.carve([128, 512], F32) for _ in range(3)]
        qr32 = [AR.carve([128, 512], F32) for _ in range(3)]
        tA = AR.carve([128, 512], F32)
        tB = AR.carve([128, 512], F32)
        q16b = [AR.carve([128, 512], BF16) for _ in range(3)]
        kloc = AR.carve([128, 4 * TOK], BF16)
        vtm = AR.carve([128, 16 * 520], BF16)
        for ti in range(16):
            p.inc("tdve", vec.memset(
                vtm[:, 520 * ti:520 * (ti + 1)]
                .rearrange("p (h j) -> p h j", j=65)[:, :, 64:65], 1.0))

        pairs = ((fA(0), fA(1)), (fA(2), fA(3)), (fB(0), fB(1)))
        wld_toks = [[], []]
        wsup_war = [0, 0]
        evac_done = [0, 0, 0]
        rope_rd = [0, 0, 0]
        q16_wr = [0, 0, 0]
        q16_rd = [0, 0, 0]
        qst_tok = [None, None, None]
        vt_cp = [0, 0]

        def emit_wload(o):
            par = o % 2
            p.wait(sync, "tpe", wsup_war[par])
            toks = [dma("A", "dA", wsup[par][:, :],
                        awt3[o].rearrange("(c p) d -> p c d", p=128))]
            if o < 20:
                toks.append(dma("A", "dA", wsupr[par][:, :],
                                awt3[40 + o].rearrange("(c p) d -> p c d", p=128)))
            wld_toks[par] = toks

        emit_wload(0)
        step = 0
        for o in range(24):
            par = o % 2
            if o + 1 < 24:
                emit_wload(o + 1)
            for ts in range(NTS):
                pr = step % 3
                step += 1
                bQ, bR = pairs[pr]
                for tk in wld_toks[par]:
                    rS.wait(ten, tk)
                p.wait(ten, "tdve", xln_done)
                p.wait(ten, "tact", evac_done[pr])
                for c in range(DC):
                    p.inc("tpe", ten.matmul(
                        bQ, wsup[par][:, 128 * c:128 * (c + 1)],
                        xln2[:, TOK * c + TS * ts: TOK * c + TS * (ts + 1)],
                        start=(c == 0), stop=(c == DC - 1)))
                if o < 20:
                    for c in range(DC):
                        p.inc("tpe", ten.matmul(
                            bR, wsupr[par][:, 128 * c:128 * (c + 1)],
                            xln2[:, TOK * c + TS * ts: TOK * c + TS * (ts + 1)],
                            start=(c == 0), stop=(c == DC - 1)))
                mm_o = p.cnt["tpe"]
                if ts == NTS - 1:
                    wsup_war[par] = mm_o
                p.wait(act, "tpe", mm_o)
                if o < 20:
                    isq = o < 16
                    ko = o - 16
                    p.wait(act, "tdve", rope_rd[pr])
                    p.inc("tact", act.activation(
                        out=qs32[pr], in_=bQ, func=AF.Identity,
                        bias=(bq8[:, o:o + 1] if isq else bk4[:, ko:ko + 1]),
                        scale=(0.125 if isq else 1.0)))
                    evac_done[pr] = p.inc("tact", act.activation(
                        out=qr32[pr], in_=bR, func=AF.Identity,
                        bias=(bqr8[:, o:o + 1] if isq else bkr4[:, ko:ko + 1]),
                        scale=(0.125 if isq else 1.0)))
                    p.wait(vec, "tact", evac_done[pr])
                    p.wait(vec, "tdve")
                    for tk in cs_tok:
                        rS.wait(vec, tk)
                    p.inc("tdve", vec.tensor_mul(tA, qs32[pr],
                                                 cosb[:, TS * ts:TS * (ts + 1)]))
                    rope_rd[pr] = p.inc("tdve", vec.tensor_mul(
                        tB, qr32[pr], sinb[:, TS * ts:TS * (ts + 1)]))
                    p.wait(vec, "tdve")
                    if isq:
                        rS.wait(vec, qst_tok[pr])
                        p.wait(vec, "tpe", q16_rd[pr])
                        q16_wr[pr] = p.inc("tdve", vec.tensor_add(
                            q16b[pr], tA, tB))
                        p.wait(sync, "tdve", q16_wr[pr])
                        qst_tok[pr] = dma("A", "dA",
                                          s_q[128 * o:128 * (o + 1),
                                              TS * ts:TS * (ts + 1)], q16b[pr])
                    else:
                        p.inc("tdve", vec.tensor_add(
                            kloc[:, TOK * ko + TS * ts: TOK * ko + TS * (ts + 1)],
                            tA, tB))
                else:
                    vo = o - 20
                    rS.wait(act, qst_tok[pr])
                    p.wait(act, "tpe", q16_rd[pr])
                    p.wait(act, "tdve", q16_wr[pr])
                    evac_done[pr] = p.inc("tact", act.activation(
                        out=q16b[pr], in_=bQ, func=AF.Identity,
                        bias=bv4[:, vo:vo + 1]))
                    for hh in range(2):
                        hd = 2 * vo + hh
                        for tc in range(4):
                            slot = tc % 2
                            bank = psBh[:, 2048 + 1024 * slot:
                                        2048 + 1024 * slot + 64]
                            p.wait(ten, "tact", evac_done[pr])
                            p.wait(ten, "tdve", vt_cp[slot])
                            q16_rd[pr] = p.inc("tpe", ten.transpose(
                                bank,
                                q16b[pr][64 * hh:64 * hh + 64, 128 * tc:128 * (tc + 1)],
                                ident64[64 * hh:64 * hh + 64, :]))
                            p.wait(vec, "tpe", q16_rd[pr])
                            ti = 4 * ts + tc
                            vt_cp[slot] = p.inc("tdve", vec.tensor_copy(
                                vtm[:, 520 * ti + 65 * hd: 520 * ti + 65 * hd + 64],
                                bank))

        p.wait(sync, "tdve")
        for ko in range(4):
            dma("A", "dA", s_kl[128 * ko:128 * (ko + 1), :], kloc[:, TOK * ko:TOK * (ko + 1)])
        for ti in range(16):
            dma("A", "dA", s_vl[128 * ti:128 * (ti + 1), :], vtm[:, 520 * ti:520 * (ti + 1)])
        dwait_all(gp)
        rg = [[0, 1], [2, 3], [4, 5], [6, 7]]
        p.inc("cc", gp.collective_compute("AllGather", OP.bypass, replica_groups=rg,
                                          ins=[s_kl.opt()], outs=[s_ka.opt()]))
        p.inc("cc", gp.collective_compute("AllGather", OP.bypass, replica_groups=rg,
                                          ins=[s_vl.opt()], outs=[s_va.opt()]))

        phase_barrier()
        AR.reset()
        kall = AR.carve([128, 8 * 2 * TOK], BF16)
        vall = AR.carve([128, 32 * 520], BF16)
        o16 = AR.carve([128, 16 * 512], BF16)
        p16 = [AR.carve([128, 2048], BF16) for _ in range(2)]
        qh = [AR.carve([128, 2048], BF16) for _ in range(2)]
        wob = [AR.carve([128, 2048], BF16) for _ in range(2)]
        xb3 = [AR.carve([128, 512], F32) for _ in range(2)]
        t323 = [AR.carve([128, 512], F32) for _ in range(2)]
        recb = AR.carve([128, 2048], F32)
        ksq = AR.carve([128, 2048], BF16)
        rc4 = w0

        p.wait(sync, "cc")
        for kvg in range(HKV):
            for r in range(2):
                rows = 512 * r + 64 * kvg
                for hp in range(2):
                    dma("A", "dA",
                        kall[64 * hp:64 * hp + 64,
                             4096 * kvg + 2048 * r: 4096 * kvg + 2048 * (r + 1)],
                        s_ka[rows:rows + 64, :])
        for i in range(32):
            dma("A", "dA", vall[:, 520 * i:520 * (i + 1)], s_va[128 * i:128 * (i + 1), :])
        kv_ld = None

        p.inc("tdve", vec.memset(mq2, 0.0))
        qn_tok = [None, None]
        for o in range(16):
            parq = o % 2
            qnb = p16[parq]
            rS.wait(sync, qn_tok[parq])
            p.wait(sync, "tact")
            qn_tok[parq] = dma("A", "dA", qnb[:, :], s_q[128 * o:128 * (o + 1), :])
            rS.wait(act, qn_tok[parq])
            p.wait(act, "tpe")
            p.inc("tact", act.activation(out=ksq, in_=qnb[:, :], func=AF.Square))
            for hh in range(2):
                H = 2 * o + hh
                p.wait(ten, "tact")
                for seg in range(4):
                    p.wait(ten, "tdve")
                    p.inc("tpe", ten.matmul(fA(2)[0:1, :],
                                            ones16[64 * hh:64 * hh + 64, :],
                                            ksq[64 * hh:64 * hh + 64,
                                                512 * seg:512 * (seg + 1)],
                                            start=True, stop=True))
                    p.wait(vec, "tpe")
                    p.inc("tdve", vec.reduce_max(out=nrow[0:1, 48:49],
                                                 in_=fA(2)[0:1, :], axis=AX.X))
                    p.wait(vec, "tdve")
                    p.inc("tdve", vec.tensor_max(mq2[0:1, H:H + 1],
                                                 mq2[0:1, H:H + 1],
                                                 nrow[0:1, 48:49]))

        p.inc("tdve", vec.memset(mk2, 0.0))
        rS.wait_all(act)
        for kv in range(HKV):
            p.wait(act, "tpe")
            p.inc("tact", act.activation(
                out=ksq[0:64, :], in_=kall[0:64, 4096 * kv + 0:4096 * kv + 2048],
                func=AF.Square))
            p.inc("tact", act.activation(
                out=ksq[64:128, :],
                in_=kall[0:64, 4096 * kv + 2048:4096 * (kv + 1)], func=AF.Square))
            p.wait(ten, "tact")
            for seg in range(8):
                base = 64 * (seg // 4)
                p.wait(ten, "tdve")
                p.inc("tpe", ten.matmul(fA(2)[0:1, :],
                                        ones16[base:base + 64, :],
                                        ksq[base:base + 64,
                                            512 * (seg % 4):512 * (seg % 4 + 1)],
                                        start=True, stop=True))
                p.wait(vec, "tpe")
                p.inc("tdve", vec.reduce_max(out=nrow[0:1, 48:49], in_=fA(2)[0:1, :],
                                             axis=AX.X))
                p.wait(vec, "tdve")
                p.inc("tdve", vec.tensor_max(mk2[0:1, kv:kv + 1], mk2[0:1, kv:kv + 1],
                                             nrow[0:1, 48:49]))
        p.wait(vec, "tdve")
        p.inc("tdve", vec.tensor_tensor(
            out=rc4[0:1, 0:32].rearrange("a (g j) -> a g j", j=4),
            in0=mq2.rearrange("a (g j) -> a g j", j=4),
            in1=mk2[0:1, :, None].broadcast_to([1, 8, 4]), op=OP.mult))
        p.wait(vec, "tdve")
        p.inc("tdve", vec.reduce_max(out=rc4[0:1, 32:33], in_=rc4[0:1, 0:32], axis=AX.X))
        p.wait(act, "tdve")
        p.inc("tact", act.activation(out=rc4[0:1, 33:34], in_=rc4[0:1, 32:33], func=AF.Ln))
        p.wait(act, "tact")
        p.inc("tact", act.activation(out=rc4[0:1, 34:35], in_=rc4[0:1, 33:34],
                                     func=AF.Exp, scale=0.5))
        p.wait(vec, "tact")
        p.inc("tdve", vec.tensor_scalar(out=rc4[0:1, 35:36], in0=rc4[0:1, 34:35],
                                        scalar1=60.0, scalar2=-1.0,
                                        op0=OP.min, op1=OP.mult))
        p.wait(sync, "tdve")
        dma("A", "dA", s_bc[3:4, 0:1], rc4[0:1, 35:36])
        dwait_all(sync)
        dma("A", "dA", negcb, s_bc[3:4, 0:1].broadcast_to([128, 1]))
        dwait_all(ten)
        dwait_all(act)
        p.wait(ten, "tdve")
        p.wait(act, "tdve")

        s_q4 = s_q.rearrange("(H p) t -> p H t", p=64)
        pv_done_dve = 0
        rc4_tok = None
        exp_prev = 0
        qh_rd = [0, 0]
        wo_mm = [0, 0]
        wo_tok = [None, None]
        st3_tok = [None, None]
        ev3 = [0, 0]

        def wo_load(dt):
            par = dt % 2
            p.wait(sync, "tpe", wo_mm[par])
            wo_tok[par] = dma("A", "dA", wob[par][:, :],
                              awt3[24 + dt].rearrange("(c p) d -> p c d", p=128))
        for qt_i in range(NTS):
            for kvg in range(HKV):
                par = kvg % 2
                p.wait(sync, "tpe", qh_rd[par])
                for hp in range(2):
                    qld = dma("A", "dA",
                              qh[par][64 * hp:64 * hp + 64, 0:1024]
                              .rearrange("p (h t) -> p h t", t=512),
                              s_q4[0:64, 4 * kvg + hp:4 * kvg + 4:2,
                                   TS * qt_i:TS * (qt_i + 1)])
                    rS.wait(ten, qld)
                e0 = exp_prev
                exp_done = {(-1, 0): e0, (-1, 1): e0}
                for ck in range(33):
                    if ck < 32:
                        kslab = kall[:, 4096 * kvg + 128 * ck:
                                     4096 * kvg + 128 * (ck + 1)]
                        pr = ck % 2
                        for half in range(2):
                            p.wait(ten, "tact", exp_done[(ck - 1, half)])
                            for hp in range(2):
                                p.inc("tpe", ten.matmul(
                                    fA(2 * half + hp),
                                    kslab[64 * hp:64 * hp + 64, :],
                                    qh[par][64 * hp:64 * hp + 64,
                                            512 * half:512 * (half + 1)],
                                    start=True, stop=True))
                            p.wait(act, "tpe")
                            exp_done[(ck, half)] = p.inc("tact", act.activation(
                                out=p16[pr][:, 1024 * half:1024 * (half + 1)],
                                in_=psA[:, 1024 * half:1024 * (half + 1)],
                                func=AF.Exp, bias=negcb[:, 0:1]))
                        if ck == 31:
                            qh_rd[par] = p.cnt["tpe"]
                    if ck >= 1:
                        cv = ck - 1
                        if cv == 0:
                            p.wait(ten, "tdve", pv_done_dve)
                        vsl = vall[:, 520 * cv + 65 * kvg: 520 * cv + 65 * (kvg + 1)]
                        for half in range(2):
                            p.wait(ten, "tact", exp_done[(cv, half)])
                            for hp in range(2):
                                h = 2 * half + hp
                                p.inc("tpe", ten.matmul(
                                    psB[0:65, 512 * h:512 * (h + 1)], vsl,
                                    p16[cv % 2][:, 512 * h:512 * (h + 1)],
                                    start=(cv == 0), stop=(cv == 31)))
                exp_prev = p.cnt["tact"]
                p.wait(act, "tpe")
                rG.wait(act, rc4_tok)
                p.inc("tact", act.activation(out=rc4[0:1, 0:2048],
                                             in_=psB[64:65, :], func=AF.Ln))
                p.wait(act, "tact")
                p.inc("tact", act.activation(out=rc4[0:1, 0:2048], in_=rc4[0:1, 0:2048],
                                             func=AF.Exp, scale=-1.0))
                p.wait(gp, "tact")
                rc4_tok = dma("B", "dB", s_bc[4:5, 0:2048], rc4[0:1, 0:2048])
                rG.wait(gp, rc4_tok)
                rbl = dma("B", "dB", recb[0:64, :],
                          s_bc[4:5, 0:2048].broadcast_to([64, 2048]))
                rG.wait(vec, rbl)
                for h in range(4):
                    H = 4 * kvg + h
                    p.inc("tdve", vec.tensor_mul(
                        o16[64 * (H % 2):64 * (H % 2) + 64,
                            512 * (H // 2):512 * (H // 2) + 512],
                        psB[0:64, 512 * h:512 * (h + 1)],
                        recb[0:64, 512 * h:512 * (h + 1)]))
                pv_done_dve = p.cnt["tdve"]
            for dt in range(DC):
                par = dt % 2
                if qt_i == 0 and dt < 2:
                    wo_load(dt)
                p.wait(gp, "tdve")
                xld = dma("B", "dB", xb3[par],
                          s_x1[128 * dt:128 * (dt + 1), TS * qt_i:TS * (qt_i + 1)])
                rS.wait(ten, wo_tok[par])
                p.wait(ten, "tdve")
                p.wait(ten, "tact", ev3[par])
                for c in range(DC):
                    p.inc("tpe", ten.matmul(
                        fA(par), wob[par][:, 128 * c:128 * (c + 1)],
                        o16[:, 512 * c:512 * (c + 1)],
                        start=(c == 0), stop=(c == DC - 1)))
                wo_mm[par] = p.cnt["tpe"]
                if dt + 2 < DC:
                    wo_load(dt + 2)
                elif qt_i + 1 < NTS:
                    wo_load(dt + 2 - DC)
                p.wait(act, "tpe", wo_mm[par])
                p.wait(act, "tdve")
                rG.wait(act, st3_tok[par])
                ev3[par] = p.inc("tact", act.activation(
                    out=t323[par], in_=fA(par), func=AF.Identity,
                    bias=bo4[:, dt:dt + 1]))
                p.wait(vec, "tact", ev3[par])
                rG.wait(vec, xld)
                p.inc("tdve", vec.tensor_add(t323[par], t323[par], xb3[par]))
                p.wait(gp, "tdve")
                st3_tok[par] = dma("B", "dB", s_x2[128 * dt:128 * (dt + 1),
                                   TS * qt_i:TS * (qt_i + 1)], t323[par])
            exp_prev = p.cnt["tact"]

        ffn(s_x2, y_out, False, "dG", "dY")

        dwait_all(sync)

    return nc


def _host_tables(core):
    half = core % 2
    pos = np.arange(2048 * half, 2048 * (half + 1), dtype=np.float32)
    j2 = np.arange(0, 64, 2, dtype=np.float32) / 64.0
    inv_freq = (1.0 / (10000.0 ** j2)).astype(np.float32)
    ang = (pos[:, None] * inv_freq[None, :]).astype(np.float32)
    cos = np.cos(ang.astype(np.float64)).astype(np.float32)
    sin = np.sin(ang.astype(np.float64)).astype(np.float32)
    cosr = np.empty((128, 2048), np.float32)
    sinr = np.empty((128, 2048), np.float32)
    for r in range(128):
        d = r % 64
        if d < 32:
            cosr[r] = cos[:, d]
            sinr[r] = -sin[:, d]
        else:
            cosr[r] = cos[:, d - 32]
            sinr[r] = sin[:, d - 32]
    return cosr, sinr


_NC_CACHE = {}


def kernel(L0_attn_x, L0_attn_Wq, L0_attn_bq, L0_attn_Wk, L0_attn_bk,
           L0_attn_Wv, L0_attn_bv, L0_attn_Wo, L0_attn_bo,
           L0_ffn_w_gate_up, L0_ffn_w_proj):
    x = np.asarray(L0_attn_x, dtype=np.float32)
    if "nc" not in _NC_CACHE:
        _NC_CACHE["nc"] = build_program()
    nc = _NC_CACHE["nc"]

    wq_tT = np.ascontiguousarray(np.asarray(L0_attn_Wq, np.float32).T)
    wk_tT = np.ascontiguousarray(np.asarray(L0_attn_Wk, np.float32).T)
    pq = np.arange(2048); pq = (pq // 64) * 64 + ((pq % 64) + 32) % 64
    pk = np.arange(512); pk = (pk // 64) * 64 + ((pk % 64) + 32) % 64
    wqr_tT = np.ascontiguousarray(wq_tT[:, pq])
    wkr_tT = np.ascontiguousarray(wk_tT[:, pk])
    bqr_r = np.ascontiguousarray(
        np.asarray(L0_attn_bq, np.float32)[pq].reshape(16, 128).T)
    bkr_r = np.ascontiguousarray(
        np.asarray(L0_attn_bk, np.float32)[pk].reshape(4, 128).T)
    wv_tT = np.ascontiguousarray(np.asarray(L0_attn_Wv, np.float32).T)
    wo_tT = np.ascontiguousarray(np.asarray(L0_attn_Wo, np.float32).T)
    bq_r = np.ascontiguousarray(np.asarray(L0_attn_bq, np.float32).reshape(16, 128).T)
    bk_r = np.ascontiguousarray(np.asarray(L0_attn_bk, np.float32).reshape(4, 128).T)
    bv_r = np.ascontiguousarray(np.asarray(L0_attn_bv, np.float32).reshape(4, 128).T)
    bo_r = np.ascontiguousarray(np.asarray(L0_attn_bo, np.float32).reshape(16, 128).T)
    wgu = np.ascontiguousarray(np.asarray(L0_ffn_w_gate_up, np.float32))
    wpr = np.ascontiguousarray(np.asarray(L0_ffn_w_proj, np.float32))
    ident = np.eye(128, dtype=ml_dtypes.bfloat16)
    ident64 = np.vstack([np.eye(64), np.eye(64)]).astype(ml_dtypes.bfloat16)

    in_maps = []
    for c in range(8):
        b, half = c // 2, c % 2
        xs = np.ascontiguousarray(x[b, 2048 * half:2048 * (half + 1), :].T)
        cosr, sinr = _host_tables(c)
        in_maps.append({
            "x": xs, "wq_t": wq_tT, "wk_t": wk_tT, "wv_t": wv_tT, "wo_t": wo_tT,
            "wqr_t": wqr_tT, "wkr_t": wkr_tT, "bqr_r": bqr_r, "bkr_r": bkr_r,
            "bq_r": bq_r, "bk_r": bk_r, "bv_r": bv_r, "bo_r": bo_r,
            "wgu": wgu, "wpr": wpr, "cosr": cosr, "sinr": sinr, "ident": ident,
            "ident64": ident64,
        })

    res = run_bass_kernel_spmd(nc, in_maps, core_ids=list(range(8)))
    out = np.empty((4, 4096, 2048), np.float32)
    for c in range(8):
        b, half = c // 2, c % 2
        out[b, 2048 * half:2048 * (half + 1), :] = res.results[c]["y"].T
    return out


# version 29 (final)
# speedup vs baseline: 1.0601x; 1.0601x over previous
import numpy as np
import ml_dtypes
import concourse.bass as bass
import concourse.mybir as mybir
from concourse.bass_utils import run_bass_kernel_spmd

F32 = mybir.dt.float32
BF16 = mybir.dt.bfloat16
U8 = mybir.dt.uint8
AF = mybir.ActivationFunctionType
OP = mybir.AluOpType
AX = mybir.AxisListType

D = 2048
DC = D // 128
TOK = 2048
TS = 512
NTS = TOK // TS
HFF = 8192
GF = HFF // 128
HKV = 8
EPS_LN = 1e-5
EPS_RMS = float(np.finfo(np.float32).eps)


class Prog:

    def __init__(self, nc):
        self.nc = nc
        self.sem_h = {}
        self.cnt = {}
        self.last_wait = {}

    def make_sems(self, stack, names):
        for n in names:
            self.sem_h[n] = stack.enter_context(self.nc.semaphore(n))
            self.cnt[n] = 0

    def inc(self, name, instr, n=1):
        self.cnt[name] += n
        instr.then_inc(self.sem_h[name], n)
        return self.cnt[name]

    def wait(self, eng, name, val=None):
        v = self.cnt[name] if val is None else val
        if v <= 0:
            return
        key = (id(eng), name)
        if self.last_wait.get(key, -1) >= v:
            return
        eng.wait_ge(self.sem_h[name], v)
        self.last_wait[key] = v


class Arena:
    def __init__(self, tensor, nbytes):
        self.t = tensor
        self.nbytes = nbytes
        self.off = 0

    def reset(self):
        self.off = 0

    def carve(self, shape, dtype):
        isz = 2 if dtype == BF16 else 4
        n = shape[1] * isz
        off = (self.off + 63) & ~63
        assert off + n <= self.nbytes, f"arena overflow {off}+{n}>{self.nbytes}"
        v = self.t[:, off:off + n].bitcast(dtype)
        if shape[0] != 128:
            v = v[0:shape[0], :]
        self.off = off + n
        return v


class Ring:

    def __init__(self, p, stack, name, n):
        self.p = p
        self.names = [f"{name}{i}" for i in range(n)]
        p.make_sems(stack, self.names)
        self.i = 0
        self.n = n

    def issue(self, eng, out, in_):
        s = self.names[self.i % self.n]
        self.i += 1
        self.p.wait(eng, s)
        v = self.p.inc(s, eng.dma_start(out=out, in_=in_), 16)
        return (s, v)

    def wait(self, eng, tok):
        if tok is not None:
            self.p.wait(eng, tok[0], tok[1])

    def wait_all(self, eng):
        for s in self.names:
            self.p.wait(eng, s)


def build_program(dump=False):
    import contextlib
    nc = bass.Bass("TRN2", target_bir_lowering=False, debug=False)
    p = Prog(nc)

    x_in = nc.dram_tensor("x", [D, TOK], F32, kind="ExternalInput").ap()
    wq_t = nc.dram_tensor("wq_t", [D, D], F32, kind="ExternalInput").ap()
    wk_t = nc.dram_tensor("wk_t", [D, 512], F32, kind="ExternalInput").ap()
    wv_t = nc.dram_tensor("wv_t", [D, 512], F32, kind="ExternalInput").ap()
    wo_t = nc.dram_tensor("wo_t", [D, D], F32, kind="ExternalInput").ap()
    wqr_t = nc.dram_tensor("wqr_t", [D, D], F32, kind="ExternalInput").ap()
    wkr_t = nc.dram_tensor("wkr_t", [D, 512], F32, kind="ExternalInput").ap()
    bqr_r = nc.dram_tensor("bqr_r", [128, 16], F32, kind="ExternalInput").ap()
    bkr_r = nc.dram_tensor("bkr_r", [128, 4], F32, kind="ExternalInput").ap()
    bq_r = nc.dram_tensor("bq_r", [128, 16], F32, kind="ExternalInput").ap()
    bk_r = nc.dram_tensor("bk_r", [128, 4], F32, kind="ExternalInput").ap()
    bv_r = nc.dram_tensor("bv_r", [128, 4], F32, kind="ExternalInput").ap()
    bo_r = nc.dram_tensor("bo_r", [128, 16], F32, kind="ExternalInput").ap()
    wgu_in = nc.dram_tensor("wgu", [2 * HFF, D], F32, kind="ExternalInput").ap()
    wpr_in = nc.dram_tensor("wpr", [D, HFF], F32, kind="ExternalInput").ap()
    cosr = nc.dram_tensor("cosr", [128, TOK], F32, kind="ExternalInput").ap()
    sinr = nc.dram_tensor("sinr", [128, TOK], F32, kind="ExternalInput").ap()
    ident_in = nc.dram_tensor("ident", [128, 128], BF16, kind="ExternalInput").ap()
    ident64_in = nc.dram_tensor("ident64", [128, 64], BF16, kind="ExternalInput").ap()
    y_out = nc.dram_tensor("y", [D, TOK], F32, kind="ExternalOutput").ap()

    dump = dump or set()
    if dump is True:
        dump = {"s_x1", "s_x2"}
    def dk(name):
        return "ExternalOutput" if name in dump else "Internal"
    s_x1 = nc.dram_tensor("s_x1", [D, TOK], F32, kind=dk("s_x1")).ap()
    s_x2 = nc.dram_tensor("s_x2", [D, TOK], F32, kind=dk("s_x2")).ap()
    s_h = nc.dram_tensor("s_h", [HFF, TOK], BF16, kind=dk("s_h")).ap()
    s_wgu = nc.dram_tensor("s_wgu", [2 * HFF, D], BF16, kind=dk("s_wgu")).ap()
    s_wpr = nc.dram_tensor("s_wpr", [HFF, D], BF16, kind=dk("s_wpr")).ap()
    s_awt = nc.dram_tensor("s_awt", [60 * D, 128], BF16).ap()
    s_q = nc.dram_tensor("s_q", [D, TOK], BF16, kind=dk("s_q")).ap()
    s_kl = nc.dram_tensor("s_kl", [512, TOK], BF16, kind=dk("s_kl")).ap()
    s_vl = nc.dram_tensor("s_vl", [TOK, HKV * 65], BF16, kind=dk("s_vl")).ap()
    s_ka = nc.dram_tensor("s_ka", [1024, TOK], BF16).ap()
    s_va = nc.dram_tensor("s_va", [2 * TOK, HKV * 65], BF16).ap()
    s_bc = nc.dram_tensor("s_bc", [8, 4096], F32, kind=dk("s_bc")).ap()

    stack = contextlib.ExitStack()
    with stack:
        bcA = stack.enter_context(nc.sbuf_tensor("bcA", [128, TOK], F32))
        bcB = stack.enter_context(nc.sbuf_tensor("bcB", [128, TOK], F32))
        consts = stack.enter_context(nc.sbuf_tensor("consts", [128, 1024], U8))
        ident = stack.enter_context(nc.sbuf_tensor("identt", [128, 128], BF16))
        ident64 = stack.enter_context(nc.sbuf_tensor("identt64", [128, 64], BF16))
        ws = stack.enter_context(nc.sbuf_tensor("ws", [128, 8192], U8))
        w0 = ws.ap().bitcast(F32)
        bq4 = consts[:, 0:64].bitcast(F32)
        bqr4 = consts[:, 512:576].bitcast(F32)
        bkr4 = consts[:, 576:592].bitcast(F32)
        bqr8 = consts[:, 592:656].bitcast(F32)
        bq8 = consts[:, 64:128].bitcast(F32)
        bk4 = consts[:, 128:144].bitcast(F32)
        bv4 = consts[:, 144:160].bitcast(F32)
        bo4 = consts[:, 160:224].bitcast(F32)
        ones16 = consts[:, 224:226].bitcast(BF16)
        ones64f = consts[:, 768:1024].bitcast(F32)
        onesf = consts[:, 228:232].bitcast(F32)
        negcb = consts[:, 232:236].bitcast(F32)
        nrow = consts[:, 256:512].bitcast(F32)
        mq2 = nrow[0:1, 0:32]
        mk2 = nrow[0:1, 32:40]

        arena_t = stack.enter_context(nc.sbuf_tensor("arena", [128, 182 * 1024], U8))
        AR = Arena(arena_t, 182 * 1024)

        psA_t = stack.enter_context(nc.psum_tensor("psA", [128, 8192], U8))
        psB_t = stack.enter_context(nc.psum_tensor("psB", [128, 8192], U8))
        psA = psA_t.ap().bitcast(F32)
        psB = psB_t.ap().bitcast(F32)
        psAh = psA_t.ap().bitcast(BF16)
        psBh = psB_t.ap().bitcast(BF16)

        def fA(i):
            return psA[:, 512 * i:512 * (i + 1)]

        def fB(i):
            return psB[:, 512 * i:512 * (i + 1)]

        p.make_sems(stack, ["tpe", "tdve", "tact", "tpool", "cc"])
        w32war = {0: 0, 1: 0, 2: 0}
        sync, ten, vec, act, gp = nc.sync, nc.tensor, nc.vector, nc.scalar, nc.gpsimd
        rS = Ring(p, stack, "rS", 12)
        rG = Ring(p, stack, "rG", 4)

        def dma(queue, sem, out, in_):
            return (rG if queue == "B" else rS).issue(gp if queue == "B" else sync,
                                                      out, in_)

        def dwait_all(eng):
            rS.wait_all(eng)
            rG.wait_all(eng)

        def phase_barrier():
            for eng in (sync, ten, vec, act, gp):
                for s in ("tpe", "tdve", "tact", "tpool", "cc"):
                    p.wait(eng, s)
                dwait_all(eng)

        dma("A", "dA", bq4, bq_r[:, :])
        dma("A", "dA", bk4, bk_r[:, :])
        dma("A", "dA", bv4, bv_r[:, :])
        dma("A", "dA", bo4, bo_r[:, :])
        dma("A", "dA", bqr4, bqr_r[:, :])
        dma("A", "dA", bkr4, bkr_r[:, :])
        dma("A", "dA", ident[:, :], ident_in[:, :])
        dma("A", "dA", ident64[:, :], ident64_in[:, :])
        dwait_all(vec)
        p.inc("tdve", vec.memset(ones16, 1.0))
        p.inc("tdve", vec.memset(onesf, 1.0))
        p.inc("tdve", vec.memset(ones64f[0:1, :], 1.0))
        p.inc("tdve", vec.tensor_scalar_mul(bq8, bq4, 0.125))
        p.inc("tdve", vec.tensor_scalar_mul(bqr8, bqr4, 0.125))

        AR.reset()
        cw32 = [AR.carve([128, 2048], F32) for _ in range(2)]
        cw16 = [AR.carve([128, 2048], BF16) for _ in range(2)]
        par = 0
        awt3 = s_awt.rearrange("(s r) d -> s r d", r=D)
        for (w_ap, ncols, st_base) in ((wq_t, D, 0), (wk_t, 512, 16),
                                       (wv_t, 512, 20), (wo_t, D, 24),
                                       (wqr_t, D, 40), (wkr_t, 512, 56)):
            for c in range(DC):
                w32v, w16v = cw32[par], cw16[par]
                dwait_all(sync)
                p.wait(sync, "tact")
                dma("A", "dA", w32v[:, 0:ncols], w_ap[128 * c:128 * (c + 1), :])
                dwait_all(act)
                p.inc("tact", act.copy(out=w16v[:, 0:ncols], in_=w32v[:, 0:ncols]))
                p.wait(sync, "tact")
                dma("A", "dA",
                    awt3[st_base:st_base + ncols // 128,
                         128 * c:128 * (c + 1), :].transpose([1, 0, 2]),
                    w16v[:, 0:ncols].rearrange("p (s d) -> p s d", d=128))
                par ^= 1

        def ln_xln(xt, ts, sqbufs, xln_slice, dsem, rb=0):
            mm_after_sq = [0, 0]
            for c in range(DC):
                sq = sqbufs[c % 2]
                if c >= 2:
                    p.wait(vec, "tpe", mm_after_sq[c % 2])
                p.inc("tdve", vec.tensor_mul(sq, xt[:, 512 * c:512 * (c + 1)],
                                             xt[:, 512 * c:512 * (c + 1)]))
                p.wait(ten, "tdve")
                p.inc("tpe", ten.matmul(fA(0)[0:1, :], onesf,
                                        xt[:, 512 * c:512 * (c + 1)],
                                        start=(c == 0), stop=(c == DC - 1)))
                mm_after_sq[c % 2] = p.inc(
                    "tpe", ten.matmul(fA(1)[0:1, :], onesf, sq,
                                      start=(c == 0), stop=(c == DC - 1)))
            p.wait(vec, "tpe")
            dwait_all(vec)
            p.inc("tdve", vec.tensor_scalar_mul(w0[0:1, 0:512], fA(0)[0:1, :], 1.0 / D))
            p.inc("tdve", vec.tensor_scalar_mul(w0[0:1, 512:1024], fA(1)[0:1, :], 1.0 / D))
            p.wait(vec, "tdve")
            p.inc("tdve", vec.tensor_mul(w0[0:1, 1024:1536], w0[0:1, 0:512], w0[0:1, 0:512]))
            p.wait(vec, "tdve")
            p.inc("tdve", vec.tensor_tensor(out=w0[0:1, 1536:2048], in0=w0[0:1, 512:1024],
                                            in1=w0[0:1, 1024:1536], op=OP.subtract))
            p.wait(vec, "tdve")
            p.inc("tdve", vec.tensor_scalar_add(w0[0:1, 1024:1536],
                                                w0[0:1, 1536:2048], EPS_LN))
            p.wait(act, "tdve")
            p.inc("tact", act.activation(out=w0[0:1, 1536:2048],
                                         in_=w0[0:1, 1024:1536], func=AF.Ln))
            p.wait(act, "tact")
            p.inc("tact", act.activation(out=w0[0:1, 1024:1536],
                                         in_=w0[0:1, 1536:2048], func=AF.Exp, scale=-0.5))
            p.wait(vec, "tact")
            p.inc("tdve", vec.tensor_mul(w0[0:1, 1536:2048],
                                         w0[0:1, 0:512], w0[0:1, 1024:1536]))
            p.wait(sync, "tdve")
            dma("A", dsem, s_bc[rb:rb + 1, 512 * ts:512 * (ts + 1)], w0[0:1, 1024:1536])
            dma("A", dsem, s_bc[rb + 1:rb + 2, 512 * ts:512 * (ts + 1)], w0[0:1, 1536:2048])
            dwait_all(sync)
            dma("A", dsem, bcA[:, 512 * ts:512 * (ts + 1)],
                s_bc[rb:rb + 1, 512 * ts:512 * (ts + 1)].broadcast_to([128, 512]))
            dma("A", dsem, bcB[:, 512 * ts:512 * (ts + 1)],
                s_bc[rb + 1:rb + 2, 512 * ts:512 * (ts + 1)].broadcast_to([128, 512]))
            dwait_all(vec)
            for c in range(DC):
                t0 = sqbufs[c % 2]
                p.wait(vec, "tdve")
                p.inc("tdve", vec.tensor_mul(t0, xt[:, 512 * c:512 * (c + 1)],
                                             bcA[:, 512 * ts:512 * (ts + 1)]))
                p.wait(vec, "tdve")
                p.inc("tdve", vec.tensor_tensor(out=xln_slice(c), in0=t0,
                                                in1=bcB[:, 512 * ts:512 * (ts + 1)],
                                                op=OP.subtract))

        G3 = lambda apx: apx.rearrange("p (g j) -> p g j", j=64)

        def q_load(qt, src_slice, dsem, war_tpe=None):
            p.wait(sync, "tact",
                   w32war.get(qt.get("w32_idx", 0), 0))
            if war_tpe is None:
                p.wait(sync, "tdve")
                p.wait(sync, "tpe")
            else:
                p.wait(sync, "tpe", war_tpe)
                p.wait(sync, "tdve", qt.get("rd_w16", 0))
            qt["_w32_tok"] = dma("A", dsem, qt["w32"][:, :], src_slice)
            qt["_war_tpe"] = war_tpe

        def q_cast(qt):
            rS.wait(act, qt["_w32_tok"])
            p.wait(act, "tdve", qt.get("rd_w16", 0))
            if qt["_war_tpe"] is not None:
                p.wait(act, "tpe", qt["_war_tpe"])
                p.wait(vec, "tpe", qt["_war_tpe"])
            p.wait(act, "tpool", qt.get("_gp_done", 0))
            w32war[qt.get("w32_idx", 0)] = p.inc(
                "tact", act.copy(out=qt["w16"][:, :], in_=qt["w32"][:, :]))
            qt["_cast"] = p.cnt["tact"]

        def q_front(qt):
            t16 = qt["aw"]
            p.wait(vec, "tact", qt["_cast"])
            p.inc("tdve", vec.reduce_sum(out=qt["ssum"], in_=G3(qt["w16"][:, :]),
                                         axis=AX.X, apply_absolute_value=True))
            p.wait(vec, "tdve")
            p.inc("tdve", vec.tensor_scalar(out=qt["s16"], in0=qt["ssum"],
                                            scalar1=1.0 / 64, scalar2=1e-8,
                                            op0=OP.mult, op1=OP.max))
            p.wait(vec, "tdve")
            rcp_done = p.inc("tdve", vec.reciprocal(out=qt["r32"], in_=qt["s16"]))
            p.wait(act, "tdve", rcp_done)
            p.wait(act, "tdve", qt.get("_s16f_war", 0))
            qt["_s16f"] = p.inc("tact", act.copy(
                out=G3(qt["s16f"][:, :]),
                in_=qt["s16"][:, :, None].broadcast_to([128, 32, 64])))
            p.wait(gp, "tdve", rcp_done)
            p.wait(gp, "tact", qt["_cast"])
            qt["_tq"] = p.inc("tpool", gp.tensor_tensor(
                out=G3(t16[:, :]), in0=G3(qt["w16"][:, :]),
                in1=qt["r32"][:, :, None].broadcast_to([128, 32, 64]), op=OP.mult))
            qt["_gp_done"] = qt["_tq"]

        def quant_back(qt):
            t16 = qt["aw"]
            p.wait(vec, "tpool", qt["_tq"])
            p.inc("tdve", vec.tensor_scalar(
                out=qt["qa"][:, :], in0=t16[:, :], scalar1=0.5, scalar2=None,
                op0=OP.is_gt))
            p.wait(vec, "tpe", qt.get("t_tpe", 0))
            p.inc("tdve", vec.tensor_scalar(
                out=qt["qb"][:, :], in0=t16[:, :], scalar1=-0.5, scalar2=None,
                op0=OP.is_lt))
            p.wait(vec, "tdve")
            p.inc("tdve", vec.tensor_sub(t16[:, :], qt["qa"][:, :],
                                         qt["qb"][:, :]))
            p.wait(vec, "tdve")
            p.wait(vec, "tact", qt["_s16f"])
            qt["_s16f_war"] = p.inc("tdve", vec.tensor_tensor(
                out=qt["qb"][:, :], in0=t16[:, :],
                in1=qt["s16f"][:, :], op=OP.mult))
            p.wait(vec, "tdve")
            p.inc("tdve", vec.tensor_sub(qt["d16"][:, :], qt["qb"][:, :],
                                         qt["w16"][:, :]))
            p.wait(vec, "tdve")
            p.inc("tdve", vec.tensor_add(qt["wq16"][:, :], qt["w16"][:, :], qt["d16"][:, :]))
            qt["rd_w16"] = p.cnt["tdve"]

        tr_state = {"evac": [0, 0]}

        def transpose_tile(src16, dst_buf, war_tpe=None, war_tok=None,
                           src_tdve=None):
            if src_tdve is None:
                src_tdve = p.cnt["tdve"]
            for half in range(2):
                bank = psBh[:, 2048 + 1024 * half: 2048 + 1024 * (half + 1)]
                p.wait(ten, "tact", tr_state["evac"][half])
                p.wait(ten, "tdve", src_tdve)
                for k in range(8):
                    blk = 8 * half + k
                    p.inc("tpe", ten.transpose(
                        bank[:, 128 * k:128 * (k + 1)],
                        src16[:, 128 * blk:128 * (blk + 1)], ident[:, :]))
                t_end = p.cnt["tpe"]
                p.wait(act, "tpe", t_end)
                if war_tpe is not None:
                    p.wait(act, "tpe", war_tpe)
                if war_tok is not None:
                    rS.wait(act, war_tok)
                tr_state["evac"][half] = p.inc("tact", act.copy(
                    out=dst_buf[:, 1024 * half:1024 * (half + 1)], in_=bank))

        def ffn(src, dst, is_ffn1, dsem, dst_sem):
            AR.reset()
            xln = AR.carve([128, DC * TOK], BF16)
            mark = AR.off
            xt = AR.carve([128, 16 * 512], F32)
            sqbufs = (AR.carve([128, 512], F32), AR.carve([128, 512], F32))
            AR.off = mark
            w32_bufs = [AR.carve([128, 2048], F32) for _ in range(3)]
            qsets = []
            for i in range(3):
                qt = {"w32": w32_bufs[i], "w32_idx": i}
                for nm in ("w16", "aw", "qa", "qb", "s16f"):
                    qt[nm] = AR.carve([128, 2048], BF16)
                qt["d16"] = qt["qa"]
                qt["wq16"] = qt["qb"]
                qt["ssum"] = AR.carve([128, 32], F32)
                qt["s16"] = AR.carve([128, 32], BF16)
                qt["r32"] = AR.carve([128, 32], F32)
                qsets.append(qt)
            lg = [AR.carve([128, 2048], BF16) for _ in range(4)]
            lw = [AR.carve([128, 2048], BF16) for _ in range(2)]
            sg32 = [AR.carve([128, 512], F32) for _ in range(2)]
            h16 = [AR.carve([128, 512], BF16) for _ in range(2)]
            sq16 = [AR.carve([128, 512], BF16) for _ in range(2)]

            src3 = src.rearrange("(c p) t -> p c t", p=128)
            phase_barrier()

            for ts in range(NTS):
                p.wait(sync, "tdve")
                p.wait(sync, "tpe")
                dma("A", dsem, xt.rearrange("p (c t) -> p c t", c=DC),
                    src3[:, :, TS * ts:TS * (ts + 1)])
                dwait_all(vec)
                dwait_all(ten)
                ln_xln(xt, ts, sqbufs,
                       lambda c, ts=ts: xln[:, TOK * c + TS * ts: TOK * c + TS * ts + TS],
                       dsem)

            lg_ready = [0, 0]
            mm_par = [0, 0]
            ssq_rd = [0, 0]
            ssq_pend = None
            sil_done = [0, 0]
            h_done = [0, 0]
            hstore_tok = [None, None]
            lw_tok = [None, None]
            lg_tok = [None] * 4
            qst = {"call": 0}

            def ssq_bank(tsx):
                pp = 32 * (tsx % 2)
                return psB[pp:pp + 1, 512 * (tsx // 2):512 * (tsx // 2) + 512]

            def q_srcs(gq):
                return (wgu_in[128 * gq:128 * (gq + 1), :],
                        wgu_in[128 * (gq + GF):128 * (gq + GF + 1), :],
                        wpr_in[128 * (gq // 4):128 * (gq // 4 + 1),
                               2048 * (gq % 4):2048 * (gq % 4 + 1)])

            def q_transpose(qt, dst, war_tpe, war_tok=None):
                transpose_tile(qt["wq16"], dst, war_tpe=war_tpe, war_tok=war_tok,
                               src_tdve=qt.get("rd_w16"))
                qt["t_tpe"] = p.cnt["tpe"]

            def emit_loads_g(gq):
                for si, sl in enumerate(q_srcs(gq)):
                    q_load(qsets[si], sl, dsem, war_tpe=qsets[si].get("t_tpe"))

            def emit_casts_g(gq):
                for si in range(3):
                    q_cast(qsets[si])

            def emit_quant_g(gq):
                qts = (qsets[0], qsets[1], qsets[2])
                for qt in qts:
                    q_front(qt)
                for qt in qts:
                    quant_back(qt)
                return qts

            def emit_transpose_g(gq, qg, qu, qw):
                gp1 = gq % 2
                q_transpose(qg, lg[2 * gp1], mm_par[gp1], war_tok=lg_tok[2 * gp1])
                p.wait(sync, "tact")
                lg_tok[2 * gp1] = dma("A", dsem, s_wgu[128 * gq:128 * (gq + 1), :],
                                      lg[2 * gp1][:, :])
                q_transpose(qu, lg[2 * gp1 + 1], mm_par[gp1],
                            war_tok=lg_tok[2 * gp1 + 1])
                p.wait(sync, "tact")
                lg_tok[2 * gp1 + 1] = dma(
                    "A", dsem, s_wgu[128 * (gq + GF):128 * (gq + GF + 1), :],
                    lg[2 * gp1 + 1][:, :])
                q_transpose(qw, lw[gp1], None, war_tok=lw_tok[gp1])
                p.wait(sync, "tact")
                e, s4 = gq // 4, gq % 4
                lw_tok[gp1] = dma(
                    "A", dsem,
                    s_wpr.rearrange("(j p) d -> p j d", p=128)[
                        :, 16 * s4:16 * (s4 + 1), 128 * e:128 * (e + 1)],
                    lw[gp1][:, :].rearrange("p (j d) -> p j d", d=128))
                lg_ready[gp1] = p.cnt["tact"]

            if is_ffn1:
                emit_loads_g(0)
                emit_casts_g(0)
                emit_loads_g(1)
                qg, qu, qw = emit_quant_g(0)
                emit_casts_g(1)
                emit_transpose_g(0, qg, qu, qw)
            for g in range(GF):
                gpar = g % 2
                lga, lgb = lg[2 * gpar], lg[2 * gpar + 1]
                if is_ffn1:
                    if g + 2 < GF:
                        emit_loads_g(g + 2)
                    nxt = tuple(qsets) if g + 1 < GF else None
                    if nxt is not None:
                        q_front(qsets[0])
                        q_front(qsets[1])
                    ld0 = ld_done = None
                else:
                    if g <= 1:
                        p.wait(sync, "tpe")
                        p.wait(sync, "tdve")
                    else:
                        p.wait(sync, "tpe", mm_par[gpar])
                    ld0 = dma("A", dsem, lga[:, :], s_wgu[128 * g:128 * (g + 1), :])
                    ld_done = dma("A", dsem, lgb[:, :],
                                  s_wgu[128 * (g + GF):128 * (g + GF + 1), :])
                for tsx in range(4):
                    pb2 = 2 * (tsx % 2)
                    gbank, ubank = fA(pb2), fA(pb2 + 1)
                    pr = tsx % 2
                    if is_ffn1:
                        p.wait(ten, "tact", lg_ready[gpar])
                    else:
                        rS.wait(ten, ld0)
                        rS.wait(ten, ld_done)
                    p.wait(ten, "tact", sil_done[pr])
                    p.wait(ten, "tdve", h_done[pr])
                    for c in range(DC):
                        for half, lbuf in ((0, lga), (1, lgb)):
                            p.inc("tpe", ten.matmul(
                                fA(pb2 + half),
                                lbuf[:, 128 * c:128 * (c + 1)],
                                xln[:, TOK * c + TS * tsx: TOK * c + TS * (tsx + 1)],
                                start=(c == 0), stop=(c == DC - 1)))
                    mm_par[gpar] = p.cnt["tpe"]
                    mm_ts = p.cnt["tpe"]
                    p.wait(act, "tpe", mm_ts)
                    p.wait(act, "tdve", h_done[pr])
                    rG.wait(act, hstore_tok[pr])
                    sil_done[pr] = p.inc("tact", act.activation(
                        out=sg32[pr], in_=gbank, func=AF.Silu))
                    p.wait(vec, "tact", sil_done[pr])
                    h_done[pr] = p.inc("tdve", vec.scalar_tensor_tensor(
                        out=h16[pr], in0=ubank, scalar=1.0, in1=sg32[pr],
                        op0=OP.mult, op1=OP.mult))
                    p.wait(act, "tdve", h_done[pr])
                    p.wait(act, "tpe", ssq_rd[pr])
                    sq_done = p.inc("tact", act.activation(
                        out=sq16[pr], in_=h16[pr], func=AF.Square))
                    if ssq_pend is not None:
                        pg, ptsx, psq = ssq_pend
                        p.wait(ten, "tact", psq)
                        ssq_rd[ptsx % 2] = p.inc(
                            "tpe", ten.matmul(ssq_bank(ptsx), ones16,
                                              sq16[ptsx % 2],
                                              start=(pg == 0),
                                              stop=(pg == GF - 1),
                                              tile_position=(0, 32 * (ptsx % 2))))
                    ssq_pend = (g, tsx, sq_done)
                    p.wait(gp, "tdve", h_done[pr])
                    hstore_tok[pr] = dma(
                        "B", dsem,
                        s_h[128 * g:128 * (g + 1), TS * tsx:TS * (tsx + 1)],
                        h16[pr])
                    if is_ffn1 and nxt is not None:
                        if tsx == 0:
                            q_front(qsets[2])
                        elif tsx == 1:
                            quant_back(qsets[0])
                            quant_back(qsets[1])
                        elif tsx == 2:
                            quant_back(qsets[2])
                            if g + 2 < GF:
                                emit_casts_g(g + 2)
                if is_ffn1 and nxt is not None:
                    emit_transpose_g(g + 1, *nxt)

            if ssq_pend is not None:
                pg, ptsx, psq = ssq_pend
                p.wait(ten, "tact", psq)
                p.inc("tpe", ten.matmul(ssq_bank(ptsx), ones16, sq16[ptsx % 2],
                                        start=False, stop=True,
                                        tile_position=(0, 32 * (ptsx % 2))))
                ssq_pend = None

            p.wait(vec, "tpe")
            dwait_all(vec)
            for tsx in range(4):
                p.inc("tdve", vec.tensor_scalar(
                    out=w0[0:1, 512 * tsx:512 * (tsx + 1)], in0=ssq_bank(tsx),
                    scalar1=1.0 / HFF, scalar2=EPS_RMS, op0=OP.mult, op1=OP.add))
            p.wait(act, "tdve")
            p.inc("tact", act.activation(out=w0[0:1, 0:2048], in_=w0[0:1, 0:2048],
                                         func=AF.Ln))
            p.wait(act, "tact")
            p.inc("tact", act.activation(out=w0[0:1, 0:2048], in_=w0[0:1, 0:2048],
                                         func=AF.Exp, scale=-0.5))
            p.wait(vec, "tact")
            p.inc("tdve", vec.tensor_scalar_mul(w0[0:1, 0:2048], w0[0:1, 0:2048], 0.5))
            p.wait(sync, "tdve")
            dma("A", dsem, s_bc[2:3, 0:2048], w0[0:1, 0:2048])
            dwait_all(sync)
            dma("A", dsem, bcA[:, :], s_bc[2:3, 0:2048].broadcast_to([128, 2048]))

            phase_barrier()
            AR.off = 0
            wph = AR.carve([128, GF * 1024], BF16)
            NHB = 6
            hb = [AR.carve([128, 512], BF16) for _ in range(NHB)]
            xb = [AR.carve([128, 512], F32) for _ in range(2)]
            t32 = [AR.carve([128, 512], F32) for _ in range(2)]
            mm_hc = [0] * 6
            stc_tok = [None, None]
            ep_done = p.cnt["tdve"]
            p.wait(ten, "tact")
            for dtg in range(2):
                p.wait(sync, "tpe")
                wph_toks = []
                for hc in range(GF):
                    wph_toks.append(dma(
                        "A", dsem, wph[:, 1024 * hc:1024 * (hc + 1)],
                        s_wpr[128 * hc:128 * (hc + 1), 1024 * dtg:1024 * (dtg + 1)]))
                for ts in range(NTS):
                    for hc in range(GF):
                        pr = hc % NHB
                        p.wait(sync, "tpe", mm_hc[pr])
                        hld = dma("A", dsem, hb[pr],
                                  s_h[128 * hc:128 * (hc + 1), TS * ts:TS * (ts + 1)])
                        rS.wait(ten, hld)
                        if ts == 0:
                            rS.wait(ten, wph_toks[hc])
                        if hc == 0:
                            p.wait(ten, "tdve", ep_done)
                        for d8 in range(8):
                            p.inc("tpe", ten.matmul(
                                (fA(d8) if d8 < 4 else fB(d8 - 4)),
                                wph[:, 1024 * hc + 128 * d8:1024 * hc + 128 * (d8 + 1)],
                                hb[pr], start=(hc == 0), stop=(hc == GF - 1)))
                        mm_hc[pr] = p.cnt["tpe"]
                    for d8 in range(8):
                        dt = 8 * dtg + d8
                        pr = d8 % 2
                        p.wait(gp, "tdve")
                        xld = dma("B", dsem, xb[pr],
                                  src[128 * dt:128 * (dt + 1), TS * ts:TS * (ts + 1)])
                        p.wait(vec, "tpe")
                        rG.wait(vec, xld)
                        rG.wait(vec, stc_tok[pr])
                        p.inc("tdve", vec.tensor_mul(
                            t32[pr], (fA(d8) if d8 < 4 else fB(d8 - 4)),
                            bcA[:, 512 * ts:512 * (ts + 1)]))
                        p.wait(vec, "tdve")
                        p.inc("tdve", vec.tensor_add(t32[pr], t32[pr], xb[pr]))
                        p.wait(gp, "tdve")
                        stc_tok[pr] = dma("B", dst_sem,
                                          dst[128 * dt:128 * (dt + 1),
                                              TS * ts:TS * (ts + 1)], t32[pr])
                    ep_done = p.cnt["tdve"]

        ffn(x_in, s_x1, True, "dF", "dF")

        phase_barrier()
        AR.reset()
        xln2 = AR.carve([128, DC * TOK], BF16)
        mark2 = AR.off
        xt2 = AR.carve([128, 16 * 512], F32)
        sqb2 = (AR.carve([128, 512], F32), AR.carve([128, 512], F32))
        x13 = s_x1.rearrange("(c p) t -> p c t", p=128)
        dwait_all(sync)
        for ts in range(NTS):
            p.wait(sync, "tpe")
            p.wait(sync, "tdve")
            dma("A", "dA", xt2.rearrange("p (c t) -> p c t", c=DC),
                x13[:, :, TS * ts:TS * (ts + 1)])
            dwait_all(vec)
            dwait_all(ten)
            ln_xln(xt2, ts, sqb2,
                   lambda c, ts=ts: xln2[:, TOK * c + TS * ts: TOK * c + TS * ts + TS],
                   "dA", rb=5)
        xln_done = p.cnt["tdve"]

        phase_barrier()
        AR.off = mark2
        wsup = [AR.carve([128, 2048], BF16) for _ in range(2)]
        wsupr = [AR.carve([128, 2048], BF16) for _ in range(2)]
        cosb = AR.carve([128, TOK], F32)
        sinb = AR.carve([128, TOK], F32)
        cs_tok = [dma("A", "dA", cosb[:, :], cosr[:, :]),
                  dma("A", "dA", sinb[:, :], sinr[:, :])]
        qs32 = [AR.carve([128, 512], F32) for _ in range(3)]
        qr32 = [AR.carve([128, 512], F32) for _ in range(3)]
        tA = AR.carve([128, 512], F32)
        tB = AR.carve([128, 512], F32)
        q16b = [AR.carve([128, 512], BF16) for _ in range(3)]
        kloc = AR.carve([128, 4 * TOK], BF16)
        vtm = AR.carve([128, 16 * 520], BF16)
        for ti in range(16):
            p.inc("tdve", vec.memset(
                vtm[:, 520 * ti:520 * (ti + 1)]
                .rearrange("p (h j) -> p h j", j=65)[:, :, 64:65], 1.0))

        pairs = ((fA(0), fA(1)), (fA(2), fA(3)), (fB(0), fB(1)))
        wld_toks = [[], []]
        wsup_war = [0, 0]
        evac_done = [0, 0, 0]
        rope_rd = [0, 0, 0]
        q16_wr = [0, 0, 0]
        q16_rd = [0, 0, 0]
        qst_tok = [None, None, None]
        vt_cp = [0, 0]

        def emit_wload(o):
            par = o % 2
            p.wait(sync, "tpe", wsup_war[par])
            toks = [dma("A", "dA", wsup[par][:, :],
                        awt3[o].rearrange("(c p) d -> p c d", p=128))]
            if o < 20:
                toks.append(dma("A", "dA", wsupr[par][:, :],
                                awt3[40 + o].rearrange("(c p) d -> p c d", p=128)))
            wld_toks[par] = toks

        emit_wload(0)
        step = 0
        for o in range(24):
            par = o % 2
            if o + 1 < 24:
                emit_wload(o + 1)
            for ts in range(NTS):
                pr = step % 3
                step += 1
                bQ, bR = pairs[pr]
                for tk in wld_toks[par]:
                    rS.wait(ten, tk)
                p.wait(ten, "tdve", xln_done)
                p.wait(ten, "tact", evac_done[pr])
                for c in range(DC):
                    p.inc("tpe", ten.matmul(
                        bQ, wsup[par][:, 128 * c:128 * (c + 1)],
                        xln2[:, TOK * c + TS * ts: TOK * c + TS * (ts + 1)],
                        start=(c == 0), stop=(c == DC - 1)))
                if o < 20:
                    for c in range(DC):
                        p.inc("tpe", ten.matmul(
                            bR, wsupr[par][:, 128 * c:128 * (c + 1)],
                            xln2[:, TOK * c + TS * ts: TOK * c + TS * (ts + 1)],
                            start=(c == 0), stop=(c == DC - 1)))
                mm_o = p.cnt["tpe"]
                if ts == NTS - 1:
                    wsup_war[par] = mm_o
                p.wait(act, "tpe", mm_o)
                if o < 20:
                    isq = o < 16
                    ko = o - 16
                    p.wait(act, "tdve", rope_rd[pr])
                    p.inc("tact", act.activation(
                        out=qs32[pr], in_=bQ, func=AF.Identity,
                        bias=(bq8[:, o:o + 1] if isq else bk4[:, ko:ko + 1]),
                        scale=(0.125 if isq else 1.0)))
                    evac_done[pr] = p.inc("tact", act.activation(
                        out=qr32[pr], in_=bR, func=AF.Identity,
                        bias=(bqr8[:, o:o + 1] if isq else bkr4[:, ko:ko + 1]),
                        scale=(0.125 if isq else 1.0)))
                    p.wait(vec, "tact", evac_done[pr])
                    p.wait(vec, "tdve")
                    for tk in cs_tok:
                        rS.wait(vec, tk)
                    p.inc("tdve", vec.tensor_mul(tA, qs32[pr],
                                                 cosb[:, TS * ts:TS * (ts + 1)]))
                    rope_rd[pr] = p.inc("tdve", vec.tensor_mul(
                        tB, qr32[pr], sinb[:, TS * ts:TS * (ts + 1)]))
                    p.wait(vec, "tdve")
                    if isq:
                        rG.wait(vec, qst_tok[pr])
                        p.wait(vec, "tpe", q16_rd[pr])
                        q16_wr[pr] = p.inc("tdve", vec.tensor_add(
                            q16b[pr], tA, tB))
                        p.wait(gp, "tdve", q16_wr[pr])
                        qst_tok[pr] = dma("B", "dB",
                                          s_q[128 * o:128 * (o + 1),
                                              TS * ts:TS * (ts + 1)], q16b[pr])
                    else:
                        p.inc("tdve", vec.tensor_add(
                            kloc[:, TOK * ko + TS * ts: TOK * ko + TS * (ts + 1)],
                            tA, tB))
                else:
                    vo = o - 20
                    rG.wait(act, qst_tok[pr])
                    p.wait(act, "tpe", q16_rd[pr])
                    p.wait(act, "tdve", q16_wr[pr])
                    evac_done[pr] = p.inc("tact", act.activation(
                        out=q16b[pr], in_=bQ, func=AF.Identity,
                        bias=bv4[:, vo:vo + 1]))
                    for hh in range(2):
                        hd = 2 * vo + hh
                        for tc in range(4):
                            slot = tc % 2
                            bank = psBh[:, 2048 + 1024 * slot:
                                        2048 + 1024 * slot + 64]
                            p.wait(ten, "tact", evac_done[pr])
                            p.wait(ten, "tdve", vt_cp[slot])
                            q16_rd[pr] = p.inc("tpe", ten.transpose(
                                bank,
                                q16b[pr][64 * hh:64 * hh + 64, 128 * tc:128 * (tc + 1)],
                                ident64[64 * hh:64 * hh + 64, :]))
                            p.wait(vec, "tpe", q16_rd[pr])
                            ti = 4 * ts + tc
                            vt_cp[slot] = p.inc("tdve", vec.tensor_copy(
                                vtm[:, 520 * ti + 65 * hd: 520 * ti + 65 * hd + 64],
                                bank))

        p.wait(sync, "tdve")
        for ko in range(4):
            dma("A", "dA", s_kl[128 * ko:128 * (ko + 1), :], kloc[:, TOK * ko:TOK * (ko + 1)])
        for ti in range(16):
            dma("A", "dA", s_vl[128 * ti:128 * (ti + 1), :], vtm[:, 520 * ti:520 * (ti + 1)])
        dwait_all(gp)
        rg = [[0, 1], [2, 3], [4, 5], [6, 7]]
        p.inc("cc", gp.collective_compute("AllGather", OP.bypass, replica_groups=rg,
                                          ins=[s_kl.opt()], outs=[s_ka.opt()]))
        p.inc("cc", gp.collective_compute("AllGather", OP.bypass, replica_groups=rg,
                                          ins=[s_vl.opt()], outs=[s_va.opt()]))

        phase_barrier()
        AR.reset()
        kall = AR.carve([128, 8 * 2 * TOK], BF16)
        vall = AR.carve([128, 32 * 520], BF16)
        o16 = AR.carve([128, 16 * 512], BF16)
        p16 = [AR.carve([128, 2048], BF16) for _ in range(2)]
        qh = [AR.carve([128, 2048], BF16) for _ in range(2)]
        wob = [AR.carve([128, 2048], BF16) for _ in range(2)]
        xb3 = [AR.carve([128, 512], F32) for _ in range(2)]
        t323 = [AR.carve([128, 512], F32) for _ in range(2)]
        recb = AR.carve([128, 2048], F32)
        ksq = AR.carve([128, 2048], BF16)
        rc4 = w0

        p.wait(sync, "cc")
        for kvg in range(HKV):
            for r in range(2):
                rows = 512 * r + 64 * kvg
                for hp in range(2):
                    dma("A", "dA",
                        kall[64 * hp:64 * hp + 64,
                             4096 * kvg + 2048 * r: 4096 * kvg + 2048 * (r + 1)],
                        s_ka[rows:rows + 64, :])
        for i in range(32):
            dma("A", "dA", vall[:, 520 * i:520 * (i + 1)], s_va[128 * i:128 * (i + 1), :])
        kv_ld = None

        p.inc("tdve", vec.memset(mq2, 0.0))
        qn_tok = [None, None]
        for o in range(16):
            parq = o % 2
            qnb = p16[parq]
            rS.wait(sync, qn_tok[parq])
            p.wait(sync, "tact")
            qn_tok[parq] = dma("A", "dA", qnb[:, :], s_q[128 * o:128 * (o + 1), :])
            rS.wait(act, qn_tok[parq])
            p.wait(act, "tpe")
            p.inc("tact", act.activation(out=ksq, in_=qnb[:, :], func=AF.Square))
            for hh in range(2):
                H = 2 * o + hh
                p.wait(ten, "tact")
                for seg in range(4):
                    p.wait(ten, "tdve")
                    p.inc("tpe", ten.matmul(fA(2)[0:1, :],
                                            ones16[64 * hh:64 * hh + 64, :],
                                            ksq[64 * hh:64 * hh + 64,
                                                512 * seg:512 * (seg + 1)],
                                            start=True, stop=True))
                    p.wait(vec, "tpe")
                    p.inc("tdve", vec.reduce_max(out=nrow[0:1, 48:49],
                                                 in_=fA(2)[0:1, :], axis=AX.X))
                    p.wait(vec, "tdve")
                    p.inc("tdve", vec.tensor_max(mq2[0:1, H:H + 1],
                                                 mq2[0:1, H:H + 1],
                                                 nrow[0:1, 48:49]))

        p.inc("tdve", vec.memset(mk2, 0.0))
        rS.wait_all(act)
        for kv in range(HKV):
            p.wait(act, "tpe")
            p.inc("tact", act.activation(
                out=ksq[0:64, :], in_=kall[0:64, 4096 * kv + 0:4096 * kv + 2048],
                func=AF.Square))
            p.inc("tact", act.activation(
                out=ksq[64:128, :],
                in_=kall[0:64, 4096 * kv + 2048:4096 * (kv + 1)], func=AF.Square))
            p.wait(ten, "tact")
            for seg in range(8):
                base = 64 * (seg // 4)
                p.wait(ten, "tdve")
                p.inc("tpe", ten.matmul(fA(2)[0:1, :],
                                        ones16[base:base + 64, :],
                                        ksq[base:base + 64,
                                            512 * (seg % 4):512 * (seg % 4 + 1)],
                                        start=True, stop=True))
                p.wait(vec, "tpe")
                p.inc("tdve", vec.reduce_max(out=nrow[0:1, 48:49], in_=fA(2)[0:1, :],
                                             axis=AX.X))
                p.wait(vec, "tdve")
                p.inc("tdve", vec.tensor_max(mk2[0:1, kv:kv + 1], mk2[0:1, kv:kv + 1],
                                             nrow[0:1, 48:49]))
        p.wait(vec, "tdve")
        p.inc("tdve", vec.tensor_tensor(
            out=rc4[0:1, 0:32].rearrange("a (g j) -> a g j", j=4),
            in0=mq2.rearrange("a (g j) -> a g j", j=4),
            in1=mk2[0:1, :, None].broadcast_to([1, 8, 4]), op=OP.mult))
        p.wait(vec, "tdve")
        p.inc("tdve", vec.reduce_max(out=rc4[0:1, 32:33], in_=rc4[0:1, 0:32], axis=AX.X))
        p.wait(act, "tdve")
        p.inc("tact", act.activation(out=rc4[0:1, 33:34], in_=rc4[0:1, 32:33], func=AF.Ln))
        p.wait(act, "tact")
        p.inc("tact", act.activation(out=rc4[0:1, 34:35], in_=rc4[0:1, 33:34],
                                     func=AF.Exp, scale=0.5))
        p.wait(vec, "tact")
        p.inc("tdve", vec.tensor_scalar(out=rc4[0:1, 35:36], in0=rc4[0:1, 34:35],
                                        scalar1=60.0, scalar2=-1.0,
                                        op0=OP.min, op1=OP.mult))
        p.wait(sync, "tdve")
        dma("A", "dA", s_bc[3:4, 0:1], rc4[0:1, 35:36])
        dwait_all(sync)
        dma("A", "dA", negcb, s_bc[3:4, 0:1].broadcast_to([128, 1]))
        dwait_all(ten)
        dwait_all(act)
        p.wait(ten, "tdve")
        p.wait(act, "tdve")

        s_q4 = s_q.rearrange("(H p) t -> p H t", p=64)
        pv_done_dve = 0
        rc4_tok = None
        exp_prev = 0
        qh_rd = [0, 0]
        wo_mm = [0, 0]
        wo_tok = [None, None]
        st3_tok = [None, None]
        ev3 = [0, 0]

        def wo_load(dt):
            par = dt % 2
            p.wait(sync, "tpe", wo_mm[par])
            wo_tok[par] = dma("A", "dA", wob[par][:, :],
                              awt3[24 + dt].rearrange("(c p) d -> p c d", p=128))
        for qt_i in range(NTS):
            for kvg in range(HKV):
                par = kvg % 2
                p.wait(sync, "tpe", qh_rd[par])
                for hp in range(2):
                    qld = dma("A", "dA",
                              qh[par][64 * hp:64 * hp + 64, 0:1024]
                              .rearrange("p (h t) -> p h t", t=512),
                              s_q4[0:64, 4 * kvg + hp:4 * kvg + 4:2,
                                   TS * qt_i:TS * (qt_i + 1)])
                    rS.wait(ten, qld)
                e0 = exp_prev
                exp_done = {(-1, 0): e0, (-1, 1): e0}
                for ck in range(33):
                    if ck < 32:
                        kslab = kall[:, 4096 * kvg + 128 * ck:
                                     4096 * kvg + 128 * (ck + 1)]
                        pr = ck % 2
                        for half in range(2):
                            p.wait(ten, "tact", exp_done[(ck - 1, half)])
                            for hp in range(2):
                                p.inc("tpe", ten.matmul(
                                    fA(2 * half + hp),
                                    kslab[64 * hp:64 * hp + 64, :],
                                    qh[par][64 * hp:64 * hp + 64,
                                            512 * half:512 * (half + 1)],
                                    start=True, stop=True))
                            p.wait(act, "tpe")
                            exp_done[(ck, half)] = p.inc("tact", act.activation(
                                out=p16[pr][:, 1024 * half:1024 * (half + 1)],
                                in_=psA[:, 1024 * half:1024 * (half + 1)],
                                func=AF.Exp, bias=negcb[:, 0:1]))
                        if ck == 31:
                            qh_rd[par] = p.cnt["tpe"]
                    if ck >= 1:
                        cv = ck - 1
                        if cv == 0:
                            p.wait(ten, "tdve", pv_done_dve)
                        vsl = vall[:, 520 * cv + 65 * kvg: 520 * cv + 65 * (kvg + 1)]
                        for half in range(2):
                            p.wait(ten, "tact", exp_done[(cv, half)])
                            for hp in range(2):
                                h = 2 * half + hp
                                p.inc("tpe", ten.matmul(
                                    psB[0:65, 512 * h:512 * (h + 1)], vsl,
                                    p16[cv % 2][:, 512 * h:512 * (h + 1)],
                                    start=(cv == 0), stop=(cv == 31)))
                exp_prev = p.cnt["tact"]
                p.wait(act, "tpe")
                rG.wait(act, rc4_tok)
                p.inc("tact", act.activation(out=rc4[0:1, 0:2048],
                                             in_=psB[64:65, :], func=AF.Ln))
                p.wait(act, "tact")
                p.inc("tact", act.activation(out=rc4[0:1, 0:2048], in_=rc4[0:1, 0:2048],
                                             func=AF.Exp, scale=-1.0))
                p.wait(gp, "tact")
                rc4_tok = dma("B", "dB", s_bc[4:5, 0:2048], rc4[0:1, 0:2048])
                rG.wait(gp, rc4_tok)
                rbl = dma("B", "dB", recb[0:64, :],
                          s_bc[4:5, 0:2048].broadcast_to([64, 2048]))
                rG.wait(vec, rbl)
                for h in range(4):
                    H = 4 * kvg + h
                    p.inc("tdve", vec.tensor_mul(
                        o16[64 * (H % 2):64 * (H % 2) + 64,
                            512 * (H // 2):512 * (H // 2) + 512],
                        psB[0:64, 512 * h:512 * (h + 1)],
                        recb[0:64, 512 * h:512 * (h + 1)]))
                pv_done_dve = p.cnt["tdve"]
            for dt in range(DC):
                par = dt % 2
                if qt_i == 0 and dt < 2:
                    wo_load(dt)
                p.wait(gp, "tdve")
                xld = dma("B", "dB", xb3[par],
                          s_x1[128 * dt:128 * (dt + 1), TS * qt_i:TS * (qt_i + 1)])
                rS.wait(ten, wo_tok[par])
                p.wait(ten, "tdve")
                p.wait(ten, "tact", ev3[par])
                for c in range(DC):
                    p.inc("tpe", ten.matmul(
                        fA(par), wob[par][:, 128 * c:128 * (c + 1)],
                        o16[:, 512 * c:512 * (c + 1)],
                        start=(c == 0), stop=(c == DC - 1)))
                wo_mm[par] = p.cnt["tpe"]
                if dt + 2 < DC:
                    wo_load(dt + 2)
                elif qt_i + 1 < NTS:
                    wo_load(dt + 2 - DC)
                p.wait(act, "tpe", wo_mm[par])
                p.wait(act, "tdve")
                rG.wait(act, st3_tok[par])
                ev3[par] = p.inc("tact", act.activation(
                    out=t323[par], in_=fA(par), func=AF.Identity,
                    bias=bo4[:, dt:dt + 1]))
                p.wait(vec, "tact", ev3[par])
                rG.wait(vec, xld)
                p.inc("tdve", vec.tensor_add(t323[par], t323[par], xb3[par]))
                p.wait(gp, "tdve")
                st3_tok[par] = dma("B", "dB", s_x2[128 * dt:128 * (dt + 1),
                                   TS * qt_i:TS * (qt_i + 1)], t323[par])
            exp_prev = p.cnt["tact"]

        ffn(s_x2, y_out, False, "dG", "dY")

        dwait_all(sync)

    return nc


def _host_tables(core):
    half = core % 2
    pos = np.arange(2048 * half, 2048 * (half + 1), dtype=np.float32)
    j2 = np.arange(0, 64, 2, dtype=np.float32) / 64.0
    inv_freq = (1.0 / (10000.0 ** j2)).astype(np.float32)
    ang = (pos[:, None] * inv_freq[None, :]).astype(np.float32)
    cos = np.cos(ang.astype(np.float64)).astype(np.float32)
    sin = np.sin(ang.astype(np.float64)).astype(np.float32)
    cosr = np.empty((128, 2048), np.float32)
    sinr = np.empty((128, 2048), np.float32)
    for r in range(128):
        d = r % 64
        if d < 32:
            cosr[r] = cos[:, d]
            sinr[r] = -sin[:, d]
        else:
            cosr[r] = cos[:, d - 32]
            sinr[r] = sin[:, d - 32]
    return cosr, sinr


_NC_CACHE = {}


def kernel(L0_attn_x, L0_attn_Wq, L0_attn_bq, L0_attn_Wk, L0_attn_bk,
           L0_attn_Wv, L0_attn_bv, L0_attn_Wo, L0_attn_bo,
           L0_ffn_w_gate_up, L0_ffn_w_proj):
    x = np.asarray(L0_attn_x, dtype=np.float32)
    if "nc" not in _NC_CACHE:
        _NC_CACHE["nc"] = build_program()
    nc = _NC_CACHE["nc"]

    wq_tT = np.ascontiguousarray(np.asarray(L0_attn_Wq, np.float32).T)
    wk_tT = np.ascontiguousarray(np.asarray(L0_attn_Wk, np.float32).T)
    pq = np.arange(2048); pq = (pq // 64) * 64 + ((pq % 64) + 32) % 64
    pk = np.arange(512); pk = (pk // 64) * 64 + ((pk % 64) + 32) % 64
    wqr_tT = np.ascontiguousarray(wq_tT[:, pq])
    wkr_tT = np.ascontiguousarray(wk_tT[:, pk])
    bqr_r = np.ascontiguousarray(
        np.asarray(L0_attn_bq, np.float32)[pq].reshape(16, 128).T)
    bkr_r = np.ascontiguousarray(
        np.asarray(L0_attn_bk, np.float32)[pk].reshape(4, 128).T)
    wv_tT = np.ascontiguousarray(np.asarray(L0_attn_Wv, np.float32).T)
    wo_tT = np.ascontiguousarray(np.asarray(L0_attn_Wo, np.float32).T)
    bq_r = np.ascontiguousarray(np.asarray(L0_attn_bq, np.float32).reshape(16, 128).T)
    bk_r = np.ascontiguousarray(np.asarray(L0_attn_bk, np.float32).reshape(4, 128).T)
    bv_r = np.ascontiguousarray(np.asarray(L0_attn_bv, np.float32).reshape(4, 128).T)
    bo_r = np.ascontiguousarray(np.asarray(L0_attn_bo, np.float32).reshape(16, 128).T)
    wgu = np.ascontiguousarray(np.asarray(L0_ffn_w_gate_up, np.float32))
    wpr = np.ascontiguousarray(np.asarray(L0_ffn_w_proj, np.float32))
    ident = np.eye(128, dtype=ml_dtypes.bfloat16)
    ident64 = np.vstack([np.eye(64), np.eye(64)]).astype(ml_dtypes.bfloat16)

    in_maps = []
    for c in range(8):
        b, half = c // 2, c % 2
        xs = np.ascontiguousarray(x[b, 2048 * half:2048 * (half + 1), :].T)
        cosr, sinr = _host_tables(c)
        in_maps.append({
            "x": xs, "wq_t": wq_tT, "wk_t": wk_tT, "wv_t": wv_tT, "wo_t": wo_tT,
            "wqr_t": wqr_tT, "wkr_t": wkr_tT, "bqr_r": bqr_r, "bkr_r": bkr_r,
            "bq_r": bq_r, "bk_r": bk_r, "bv_r": bv_r, "bo_r": bo_r,
            "wgu": wgu, "wpr": wpr, "cosr": cosr, "sinr": sinr, "ident": ident,
            "ident64": ident64,
        })

    res = run_bass_kernel_spmd(nc, in_maps, core_ids=list(range(8)))
    out = np.empty((4, 4096, 2048), np.float32)
    for c in range(8):
        b, half = c // 2, c % 2
        out[b, 2048 * half:2048 * (half + 1), :] = res.results[c]["y"].T
    return out
